# version 35
# baseline (speedup 1.0000x reference)
import numpy as np
from contextlib import ExitStack

import concourse.bass as bass
import concourse.mybir as mybir
from concourse.bass_utils import run_bass_kernel_spmd

F32 = mybir.dt.float32
F16 = mybir.dt.float16
BF16 = mybir.dt.bfloat16
FP8 = mybir.dt.float8e4
I32 = mybir.dt.int32
AF = mybir.ActivationFunctionType
ALU = mybir.AluOpType
AX = mybir.AxisListType

D = 1024
H = 4096
C = 128
KT = D // 128
EPS_LN = 1e-5
EPS_DEN = 1e-6
EPS_Q = 1e-5
QB = 127.0

ENGS = ("tensor", "vector", "scalar", "gpsimd", "sync")
SEMS = ENGS + ("gdma",)


class Em:

    def __init__(self, nc, es):
        self.nc = nc
        self.sem = {n: es.enter_context(nc.semaphore("S_" + n)) for n in SEMS}
        self.cnt = {n: 0 for n in SEMS}
        self.seen = {a: {b: 0 for b in SEMS} for a in SEMS}
        self.ops = {n: [] for n in ENGS}
        self.strict_scalar = True

    def __call__(self, eng, meth, *args, deps=(), inc=None, **kw):
        sem = eng
        if eng == "gdma":
            eng = "gpsimd"
        dl = [d for d in deps if d is not None]
        if sem in ("vector", "scalar", "gpsimd") and self.cnt[sem] > 0:
            dl.append((sem, self.cnt[sem]))
        if sem in ("sync", "gdma") and self.cnt[sem] > 0:
            dl.append((sem, self.cnt[sem]))
        if (self.strict_scalar and sem in ("scalar", "gpsimd")
                and self.cnt["vector"] > 0):
            dl.append(("vector", self.cnt["vector"]))
        waits = []
        for dn, dv in dl:
            if dv > self.seen[eng][dn]:
                waits.append((dn, dv))
                self.seen[eng][dn] = dv
        if inc is None:
            inc = 16 if meth.startswith("dma") else (1 if eng != "tensor" else 0)
        self.ops[eng].append((meth, args, kw, waits, inc, sem))
        if inc:
            self.cnt[sem] += inc
            return (sem, self.cnt[sem])
        return None

    def barrier_tokens(self):
        return [(n, self.cnt[n]) for n in SEMS]

    def final_wait(self, eng):
        self.ops[eng].append(("__wait__", (), {},
                              [(n, self.cnt[n]) for n in ("sync", "gdma")], 0,
                              eng))

    def flush(self, block):
        for name in ENGS:
            ops = self.ops[name]

            def mk(ops=ops):
                def run(h):
                    for meth, args, kw, waits, inc, sem in ops:
                        for dn, dv in waits:
                            h.wait_ge(self.sem[dn], dv)
                        if meth == "__wait__":
                            continue
                        r = getattr(h, meth)(*args, **kw)
                        if inc:
                            r.then_inc(self.sem[sem], inc)
                return run
            getattr(block, name)(mk())


def build_nc(n_main=32, n_warm=4):
    nc = bass.Bass("TRN2", target_bir_lowering=False)
    n_tot = n_main + n_warm
    n_tok_in = n_tot * C
    n_tok_out = n_main * C

    x_d = nc.dram_tensor("x", [n_tok_in, D], F32, kind="ExternalInput").ap()
    wqt_d = nc.dram_tensor("wqt", [D, D], F32, kind="ExternalInput").ap()
    wkt_d = nc.dram_tensor("wkt", [D, D], F32, kind="ExternalInput").ap()
    wvt_d = nc.dram_tensor("wvt", [D, D], F32, kind="ExternalInput").ap()
    bq_d = nc.dram_tensor("bq", [1, D], F32, kind="ExternalInput").ap()
    bk_d = nc.dram_tensor("bk", [1, D], F32, kind="ExternalInput").ap()
    bv_d = nc.dram_tensor("bv", [1, D], F32, kind="ExternalInput").ap()
    gam_d = nc.dram_tensor("gam", [1, 1], F32, kind="ExternalInput").ap()
    wmask_d = nc.dram_tensor("wmask", [128, 1], F32, kind="ExternalInput").ap()
    w1t_d = nc.dram_tensor("w1t", [D, H], F32, kind="ExternalInput").ap()
    b1_d = nc.dram_tensor("b1", [1, H], F32, kind="ExternalInput").ap()
    w2t_d = nc.dram_tensor("w2t", [H, D], F32, kind="ExternalInput").ap()
    b2_d = nc.dram_tensor("b2", [1, D], F32, kind="ExternalInput").ap()
    y_d = nc.dram_tensor("y", [n_tok_out, D], F32, kind="ExternalOutput").ap()
    x2s_d = nc.dram_tensor("x2scr", [n_tok_out, D], F32).ap()
    xqs_d = nc.dram_tensor("xqscr", [n_tok_out, D], BF16).ap()
    w1q_d = nc.dram_tensor("w1q", [D, H], FP8).ap()
    w2q_d = nc.dram_tensor("w2q", [H, D], FP8).ap()

    es = ExitStack()
    with es:
        block = es.enter_context(nc.Block())
        E = Em(nc, es)

        psb = lambda name, shape, dt=F32: es.enter_context(nc.sbuf_tensor(name, shape, dt))
        onesrow = psb("onesrow", [1, 128])
        onescol = psb("onescol", [128, 1])
        gvec = psb("gvec", [128, 1])
        givec = psb("givec", [128, 1])
        gcvec = psb("gcvec", [128, 1])
        maskvec = psb("maskvec", [128, 1])
        ws1b = psb("ws1b", [128, 1])
        hws1b = psb("hws1b", [128, 2])
        ws2b = psb("ws2b", [128, 1])
        hws2b = psb("hws2b", [128, 2])
        w2s127 = psb("w2s127", [128, 1])
        iden = psb("iden", [128, 128])
        idenb = psb("idenb", [128, 128], BF16)
        maskUT = psb("maskUT", [128, 128])
        gamsc = psb("gamsc", [1, 1])
        epsv = psb("epsv", [128, 1])
        stats = psb("stats", [128, 3 * n_main])
        r1 = psb("r1", [128, 16])
        r2 = psb("r2", [128, 16])

        esA = ExitStack()
        pA = esA.enter_context(nc.psum_tensor("pA", [128, 1024], F32))
        pB = esA.enter_context(nc.psum_tensor("pB", [128, 1024], F32))
        pC = esA.enter_context(nc.psum_tensor("pC", [128, 1024], F32))
        pD = esA.enter_context(nc.psum_tensor("pD", [128, 1024], F32))

        gbq = esA.enter_context(nc.sbuf_tensor("gbq", [128, D], F32))
        gbk = esA.enter_context(nc.sbuf_tensor("gbk", [128, D], F32))
        bvb = esA.enter_context(nc.sbuf_tensor("bvb", [128, D], F32))

        esI = ExitStack()
        isb = lambda name, shape, dt=F32: esI.enter_context(nc.sbuf_tensor(name, shape, dt))
        gamrow = isb("gamrow", [1, 128])
        zerrow = isb("zerrow", [1, 128])
        grow = isb("grow", [1, 128])
        girow = isb("girow", [1, 128])
        iotat = isb("iotat", [128, 128])
        bqrow = isb("bqrow", [1, D])
        bkrow = isb("bkrow", [1, D])
        bvrow = isb("bvrow", [1, D])

        t_in0 = E("sync", "dma_start", out=gamsc[:], in_=gam_d[:])
        E("sync", "dma_start", out=bqrow[:], in_=bq_d[:])
        E("sync", "dma_start", out=bkrow[:], in_=bk_d[:])
        t_rows = E("sync", "dma_start", out=bvrow[:], in_=bv_d[:])
        t_in1 = E("sync", "dma_start", out=maskvec[:], in_=wmask_d[:])

        tg_io = E("gpsimd", "iota", out=iotat[:], pattern=[[1, 128]], base=0,
                  channel_multiplier=-1, allow_small_or_imprecise_dtypes=True)

        E("vector", "memset", onesrow[:], 1.0)
        E("vector", "memset", onescol[:], 1.0)
        E("vector", "memset", zerrow[:], 0.0)
        E("vector", "memset", epsv[:], EPS_LN)
        E("vector", "tensor_scalar", out=maskUT[:], in0=iotat[:], scalar1=0.0,
          scalar2=None, op0=ALU.is_ge, deps=[tg_io])
        E("vector", "tensor_scalar", out=iden[:], in0=iotat[:], scalar1=0.0,
          scalar2=None, op0=ALU.is_equal)
        E("vector", "tensor_copy", out=idenb[:], in_=iden[:])
        E("vector", "memset", gamrow[:], 1.0)
        E("vector", "tensor_scalar", out=gamrow[:], in0=gamrow[:],
          scalar1=gamsc[0:1, 0:1], scalar2=None, op0=ALU.mult, deps=[t_in0])
        E("vector", "tensor_tensor_scan", out=grow[:], data0=gamrow[:],
          data1=zerrow[:], initial=1.0, op0=ALU.mult, op1=ALU.add)
        E("vector", "reciprocal", out=r1[0:1, 0:1], in_=gamsc[0:1, 0:1])
        E("vector", "memset", gamrow[:], 1.0)
        E("vector", "tensor_scalar", out=gamrow[:], in0=gamrow[:],
          scalar1=r1[0:1, 0:1], scalar2=None, op0=ALU.mult)
        tv_rows = E("vector", "tensor_tensor_scan", out=girow[:], data0=gamrow[:],
                    data1=zerrow[:], initial=1.0, op0=ALU.mult, op1=ALU.add)

        tvgci = E("vector", "tensor_scalar", out=gamrow[:], in0=girow[:],
                  scalar1=grow[0:1, 127:128], scalar2=None, op0=ALU.mult)
        E("tensor", "transpose", pD[:, 0:1], grow[:], iden[0:1, 0:1],
          deps=[tv_rows, tvgci])
        E("tensor", "transpose", pD[:, 128:129], gamrow[:], iden[0:1, 0:1])
        tt_g = E("tensor", "matmul", pD[:, 256:257], onesrow[:],
                 grow[0:1, 127:128], start=True, stop=True, inc=1)
        E("vector", "tensor_copy", out=gvec[:], in_=pD[:, 0:1], deps=[tt_g])
        E("vector", "tensor_copy", out=givec[:], in_=pD[:, 128:129])
        tv_g = E("vector", "tensor_copy", out=gcvec[:], in_=pD[:, 256:257])
        E("vector", "reciprocal", out=r1[:, 15:16], in_=gcvec[:])
        E("vector", "tensor_scalar", out=maskUT[:], in0=maskUT[:],
          scalar1=r1[:, 15:16], scalar2=None, op0=ALU.mult)

        for hh in range(2):
            sl = slice(hh * 512, (hh + 1) * 512)
            E("tensor", "matmul", pA[:, sl], grow[:], bqrow[:, sl], start=True,
              stop=True, deps=[tv_g, t_rows])
            E("tensor", "matmul", pB[:, sl], gamrow[:], bkrow[:, sl], start=True, stop=True)
            tt_b = E("tensor", "matmul", pC[:, sl], onesrow[:], bvrow[:, sl],
                     start=True, stop=True, inc=1)
        E("vector", "tensor_copy", out=gbq[:], in_=pA[:], deps=[tt_b])
        E("vector", "tensor_copy", out=gbk[:], in_=pB[:])
        tv_bias = E("vector", "tensor_copy", out=bvb[:], in_=pC[:])
        barrI = E.barrier_tokens()
        esI.close()

        sbA = lambda name, shape, dt=F32: esA.enter_context(nc.sbuf_tensor(name, shape, dt))
        WQTH = sbA("WQTH", [128, KT * D], F16)
        WQTL = sbA("WQTL", [128, KT * D], F16)
        WKTH = sbA("WKTH", [128, KT * D], F16)
        WKTL = sbA("WKTL", [128, KT * D], F16)
        WVTH = sbA("WVTH", [128, KT * D], F16)
        WVTL = sbA("WVTL", [128, KT * D], F16)
        S_sb = sbA("S_sb", [128, KT * D])
        z_sb = sbA("z_sb", [128, KT])
        xin = [sbA("xin0", [128, D]), sbA("xin1", [128, D])]
        bufA = sbA("bufA", [128, D])
        bufB = sbA("bufB", [128, D])
        dmp2 = sbA("dmp2", [128, D])
        xhatTh = sbA("xhatTh", [128, D], F16)
        xhatTl = sbA("xhatTl", [128, D], F16)
        qt = sbA("qt", [128, D])
        kt_ = sbA("kt", [128, D])
        kth16 = sbA("kth16", [128, D], F16)
        ktl16 = sbA("ktl16", [128, D], F16)
        vt = sbA("vt", [128, D])
        vh16 = sbA("vh16", [128, D], F16)
        vl16 = sbA("vl16", [128, D], F16)
        qf = sbA("qf", [128, D])
        kf = sbA("kf", [128, D])
        pmask = sbA("pmask", [128, 128])
        pmh = sbA("pmh", [128, 128], F16)
        pml = sbA("pml", [128, 128], F16)
        ones16 = sbA("ones16", [128, 1], F16)
        wstg = sbA("wstg", [128, 1024])
        wg1 = sbA("wg1", [128, 512], FP8)
        wg2 = sbA("wg2", [128, 512], FP8)
        absacc = sbA("absacc", [128, 64])
        wqr = sbA("wqr", [128, 4])
        tqA = esA.enter_context(nc.sbuf_tensor("tqA", [128, D], I32))
        xqA = sbA("xqA", [128, D], BF16)

        E("vector", "memset", S_sb[:], 0.0, deps=barrI)
        tv_z0 = E("vector", "memset", z_sb[:], 0.0)
        t_x0 = E("gdma", "dma_start", out=xin[0][:], in_=x_d[0:128, :], deps=barrI)
        E("vector", "tensor_copy", out=ones16[:], in_=onescol[:], deps=barrI)
        wsplit = {"rd": [None, None]}
        def load_w_pair(dram, WH, WL):
            last = None
            for kt in range(KT):
                h = kt % 2
                buf = wstg if h == 0 else qt
                td = E("sync" if h == 0 else "gdma", "dma_start", out=buf[:],
                       in_=dram[kt * 128:(kt + 1) * 128, :],
                       deps=[wsplit["rd"][h]] + barrI)
                sl = slice(kt * 1024, (kt + 1) * 1024)
                E("vector", "tensor_copy", out=WH[:, sl], in_=buf[:], deps=[td])
                last = E("vector", "tensor_tensor", out=WL[:, sl], in0=buf[:],
                         in1=WH[:, sl], op=ALU.subtract)
                wsplit["rd"][h] = last
            return last
        t_wk = load_w_pair(wkt_d, WKTH, WKTL)
        t_wv = load_w_pair(wvt_d, WVTH, WVTL)
        t_wq = load_w_pair(wqt_d, WQTH, WQTL)

        tok = {
            "xq_ev": None, "xq_odd": None,
            "v15": tv_bias, "v16": None, "v17": None, "v21": tv_z0,
            "num": None, "den": None, "P": None,
            "q": None, "k": None, "v": t_wv,
            "trq": None, "trk": None, "trx": None,
            "L": [None] * KT, "z": None,
            "v19": None, "s5": None, "x2dma": None,
            "v9": None, "v10": None, "v11": None, "v12": None, "v13": None,
            "v14": tv_bias, "s3": None, "s4": None, "pDfree": tv_bias,
            "v20": [tv_z0] * KT, "v20last": tv_z0,
            "xin_rd": [[t_x0], []],
            "v8": None, "s1": None, "tskh": None, "xqdma": None,
        }

        def ln1(c, dep_dma):
            g = c % 2
            E("scalar", "activation", out=dmp2[:], in_=xin[g][:],
              func=AF.Copy, bias=0.0, scale=1.0,
              accum_out=r1[:, 0:1], deps=[dep_dma, tok["s5"]])
            tok["s1"] = E("scalar", "activation", out=bufB[:], in_=xin[g][:],
                          func=AF.Square, bias=0.0, scale=1.0,
                          accum_out=r1[:, 1:2], deps=[dep_dma, tok["v11"],
                                                      tok["trk"]])
            E("vector", "tensor_scalar", out=r1[:, 2:3], in0=r1[:, 0:1],
              scalar1=1.0 / D, scalar2=None, op0=ALU.mult, deps=[tok["s1"]])
            E("vector", "tensor_scalar", out=r1[:, 3:4], in0=r1[:, 1:2],
              scalar1=1.0 / D, scalar2=None, op0=ALU.mult, deps=[tok["s1"]])
            E("vector", "tensor_scalar", out=r1[:, 4:5], in0=r1[:, 2:3],
              scalar1=r1[:, 2:3], scalar2=None, op0=ALU.mult)
            tv = E("vector", "tensor_tensor", out=r1[:, 5:6], in0=r1[:, 3:4],
                   in1=r1[:, 4:5], op=ALU.subtract)
            ts = E("scalar", "activation", out=r1[:, 6:7], in_=r1[:, 5:6],
                   func=AF.Sqrt, bias=epsv[:, 0:1], scale=1.0, deps=[tv])
            E("vector", "reciprocal", out=r1[:, 7:8], in_=r1[:, 6:7], deps=[ts])
            E("vector", "tensor_scalar", out=r1[:, 8:9], in0=r1[:, 2:3],
              scalar1=r1[:, 7:8], scalar2=-1.0, op0=ALU.mult, op1=ALU.mult)
            tok["v8"] = E("vector", "tensor_scalar", out=bufA[:], in0=xin[g][:],
                          scalar1=r1[:, 7:8], scalar2=r1[:, 8:9], op0=ALU.mult,
                          op1=ALU.add, deps=[tok["trq"], tok["trx"]])
            tok["xin_rd"][g] = [tok["v8"]]

        def front(c):
            for dt_ in range(KT):
                tok_t = E("tensor", "transpose", pD[:, dt_ * 128:(dt_ + 1) * 128],
                          bufA[:, dt_ * 128:(dt_ + 1) * 128], iden[:],
                          deps=[tok["v8"], tok["pDfree"]] if dt_ == 0 else (),
                          inc=1 if dt_ == KT - 1 else 0)
            tok["trx"] = tok_t
            tsh = E("scalar", "copy", out=xhatTh[:], in_=pD[:],
                    deps=[tok["trx"], tok["v"]])
            tok["v9"] = E("vector", "tensor_tensor", out=xhatTl[:], in0=pD[:],
                          in1=xhatTh[:], op=ALU.subtract, deps=[tsh])
            tok["pDfree"] = tok["v9"]

        def ln2_stats(c):
            m = c - n_warm
            g = c % 2
            E("scalar", "activation", out=dmp2[:], in_=xin[g][:],
              func=AF.Copy, bias=0.0, scale=1.0,
              accum_out=r2[:, 2:3], deps=[tok["v19"]])
            tok["s5"] = E("scalar", "activation", out=dmp2[:], in_=xin[g][:],
                          func=AF.Square, bias=0.0, scale=1.0,
                          accum_out=r2[:, 3:4], deps=[tok["v19"]])
            E("vector", "tensor_reduce", out=r2[:, 4:5], in_=xin[g][:],
              axis=AX.X, op=ALU.max)
            E("vector", "tensor_reduce", out=r2[:, 5:6], in_=xin[g][:],
              axis=AX.X, op=ALU.min, negate=True)
            E("vector", "tensor_scalar", out=r2[:, 6:7], in0=r2[:, 2:3],
              scalar1=1.0 / D, scalar2=None, op0=ALU.mult, deps=[tok["s5"]])
            E("vector", "tensor_scalar", out=r2[:, 7:8], in0=r2[:, 3:4],
              scalar1=1.0 / D, scalar2=None, op0=ALU.mult, deps=[tok["s5"]])
            E("vector", "tensor_scalar", out=r2[:, 8:9], in0=r2[:, 6:7],
              scalar1=r2[:, 6:7], scalar2=None, op0=ALU.mult)
            tvv = E("vector", "tensor_tensor", out=r2[:, 9:10], in0=r2[:, 7:8],
                    in1=r2[:, 8:9], op=ALU.subtract)
            tss = E("scalar", "activation", out=r2[:, 10:11], in_=r2[:, 9:10],
                    func=AF.Sqrt, bias=epsv[:, 0:1], scale=1.0, deps=[tvv])
            E("vector", "reciprocal", out=r2[:, 11:12], in_=r2[:, 10:11],
              deps=[tss])
            E("vector", "tensor_tensor", out=r2[:, 12:13], in0=r2[:, 4:5],
              in1=r2[:, 6:7], op=ALU.subtract)
            E("vector", "tensor_tensor", out=r2[:, 13:14], in0=r2[:, 5:6],
              in1=r2[:, 6:7], op=ALU.add)
            E("vector", "tensor_tensor", out=r2[:, 12:13], in0=r2[:, 12:13],
              in1=r2[:, 13:14], op=ALU.max)
            E("vector", "tensor_scalar", out=r2[:, 12:13], in0=r2[:, 12:13],
              scalar1=r2[:, 11:12], scalar2=EPS_Q, op0=ALU.mult, op1=ALU.max)
            E("vector", "reciprocal", out=r2[:, 14:15], in_=r2[:, 12:13])
            E("vector", "tensor_scalar", out=r2[:, 14:15], in0=r2[:, 14:15],
              scalar1=QB, scalar2=None, op0=ALU.mult)
            E("vector", "tensor_scalar", out=stats[:, 3 * m:3 * m + 1],
              in0=r2[:, 11:12], scalar1=r2[:, 14:15], scalar2=None,
              op0=ALU.mult)
            E("vector", "tensor_scalar", out=stats[:, 3 * m + 1:3 * m + 2],
              in0=r2[:, 6:7], scalar1=stats[:, 3 * m:3 * m + 1], scalar2=-1.0,
              op0=ALU.mult, op1=ALU.mult)
            tvs = E("vector", "tensor_scalar",
                    out=stats[:, 3 * m + 2:3 * m + 3],
                    in0=r2[:, 12:13], scalar1=1.0 / QB, scalar2=None,
                    op0=ALU.mult)
            sca = stats[:, 3 * m:3 * m + 1]
            scb = stats[:, 3 * m + 1:3 * m + 2]
            sv = E("scalar", "activation", out=vt[:], in_=xin[g][:],
                   func=AF.Identity, scale=sca, bias=scb, deps=[tvs])
            st = E("scalar", "activation", out=tqA[:], in_=xin[g][:],
                   func=AF.Identity, scale=sca, bias=scb)
            E("vector", "scalar_tensor_tensor", out=qt[:], in0=tqA[:],
              scalar=0.5, in1=vt[:], op0=ALU.add, op1=ALU.is_le,
              deps=[sv, st, wq_state["rd"][1]])
            E("vector", "scalar_tensor_tensor", out=kt_[:], in0=tqA[:],
              scalar=-0.5, in1=vt[:], op0=ALU.add, op1=ALU.is_ge,
              deps=[tok["tskh"]])
            tst2 = E("vector", "scalar_tensor_tensor", out=vt[:], in0=qt[:],
                     scalar=1.0, in1=tqA[:], op0=ALU.mult, op1=ALU.add)
            wq_state["rd"][1] = tst2
            txq = E("vector", "tensor_tensor", out=xqA[:], in0=vt[:],
                    in1=kt_[:], op=ALU.subtract, deps=[tok["xqdma"]])
            tok["xqdma"] = E("gdma", "dma_start",
                             out=xqs_d[m * 128:(m + 1) * 128, :],
                             in_=xqA[:], deps=[txq])
            tok["xin_rd"][g] = [tvs, st, tok["x2dma"]]

        def wq_sub(pt):
            if wq_state.get("ws_pending") and pt == 0:
                wq_state["ws_pending"] = False
                wq_ws_compute()
            kindw, parts = wq_state["mid"]
            if kindw is None or pt >= len(parts):
                return
            items = parts[pt]
            if not items:
                return
            if kindw == "p1":
                wq_pass1_slice(items)
            else:
                wq_pass2_slice(items)

        def chunk(c):
            warm = c < n_warm
            m = c - n_warm
            g = c % 2
            if c + 1 < n_tot:
                gn = (c + 1) % 2
                tok_dma_next = E("gdma", "dma_start", out=xin[gn][:],
                                 in_=x_d[(c + 1) * 128:(c + 2) * 128, :],
                                 deps=tok["xin_rd"][gn])
            else:
                tok_dma_next = None

            def proj(WH, WL, wtok, pdst, dep_free):
                t = None
                for hh in range(2):
                    for kk in range(KT):
                        co = kk * D + hh * 512
                        for ph, (lh, rh) in enumerate(
                                ((xhatTh, WH), (xhatTl, WH), (xhatTh, WL))):
                            t = E("tensor", "matmul",
                                  pdst[:, hh * 512:(hh + 1) * 512],
                                  lh[:, kk * 128:(kk + 1) * 128],
                                  rh[:, co:co + 512],
                                  start=(kk == 0 and ph == 0),
                                  stop=(kk == KT - 1 and ph == 2),
                                  deps=([tok["v9"], wtok] + dep_free)
                                  if (hh == 0 and kk == 0 and ph == 0) else (),
                                  inc=1 if (hh == 1 and kk == KT - 1 and ph == 2) else 0)
                return t

            if not warm:
                tok["q"] = proj(WQTH, WQTL, t_wq, pA, [tok["xq_ev"], tok["v19"]])
                tok["v10"] = E("vector", "scalar_tensor_tensor", out=bufA[:],
                               in0=pA[:], scalar=gvec[:, 0:1], in1=gbq[:],
                               op0=ALU.mult, op1=ALU.add,
                               deps=[tok["q"], tok["trx"]])
            tok["k"] = proj(WKTH, WKTL, t_wk, pB, [tok["v16"], tok["v21"]])
            tok["v11"] = E("vector", "scalar_tensor_tensor", out=bufB[:],
                           in0=pB[:], scalar=givec[:, 0:1], in1=gbk[:],
                           op0=ALU.mult, op1=ALU.add,
                           deps=[tok["k"], tok["s1"]])
            tok["s4"] = E("scalar", "activation", out=kt_[:], in_=bufB[:],
                          func=AF.Relu, bias=0.0, scale=1.0,
                          deps=[tok["v11"]])
            tok["v"] = proj(WVTH, WVTL, t_wv, pC, [tok["xq_odd"], tok["v12"]])
            tvt = E("vector", "tensor_tensor", out=vt[:], in0=pC[:],
                    in1=bvb[:], op=ALU.add, deps=[tok["v"], tok["L"][KT - 1]])
            if not warm:
                for dt_ in range(KT):
                    tok_t = E("tensor", "transpose", pD[:, dt_ * 128:(dt_ + 1) * 128],
                              bufA[:, dt_ * 128:(dt_ + 1) * 128], iden[:],
                              deps=[tok["v10"], tok["pDfree"], tok["den"]]
                              if dt_ == 0 else (),
                              inc=1 if dt_ == KT - 1 else 0)
                tok["trq"] = tok_t
                tok["v14"] = E("vector", "tensor_scalar", out=qf[:], in0=pD[:],
                               scalar1=0.0, scalar2=None, op0=ALU.max,
                               deps=[tok["trq"]])
                tok["pDfree"] = tok["v14"]
                for dt_ in range(KT):
                    tok_t = E("tensor", "transpose", pB[:, dt_ * 128:(dt_ + 1) * 128],
                              bufB[:, dt_ * 128:(dt_ + 1) * 128], iden[:],
                              deps=[tok["v11"]] if dt_ == 0 else (),
                              inc=1 if dt_ == KT - 1 else 0)
                tok["trk"] = tok_t
                tok["v15"] = E("scalar", "activation", out=kf[:], in_=pB[:, 0:1024],
                               func=AF.Relu, bias=0.0, scale=1.0,
                               deps=[tok["trk"], tok["P"]])
            tskh = E("scalar", "copy", out=kth16[:], in_=kt_[:], deps=[tok["z"]])
            tok["tskh"] = tskh
            tsvh = E("scalar", "copy", out=vh16[:], in_=vt[:],
                     deps=[tok["L"][KT - 1]])
            if not warm:
                for kk in range(KT):
                    tok_t = E("tensor", "matmul", pB[:, 0:128],
                              kf[:, kk * 128:(kk + 1) * 128],
                              qf[:, kk * 128:(kk + 1) * 128],
                              start=(kk == 0), stop=(kk == KT - 1),
                              deps=[tok["v15"], tok["v14"], tok["v11"]] if kk == 0 else (),
                              inc=1 if kk == KT - 1 else 0)
                tok["P"] = tok_t
                E("vector", "tensor_tensor", out=pmask[:],
                  in0=pB[:, 0:128], in1=maskUT[:], op=ALU.mult,
                  deps=[tok["P"]])
                tsph = E("scalar", "copy", out=pmh[:], in_=pmask[:])
                tok["v16"] = E("vector", "tensor_tensor", out=pml[:],
                               in0=pmask[:], in1=pmh[:], op=ALU.subtract,
                               deps=[tsph])
            tok["v13"] = E("vector", "tensor_tensor", out=ktl16[:], in0=kt_[:],
                           in1=kth16[:], op=ALU.subtract,
                           deps=[tskh, tok["z"]])
            tok["v12"] = E("vector", "tensor_tensor", out=vl16[:], in0=vt[:],
                           in1=vh16[:], op=ALU.subtract, deps=[tsvh])
            if c + 1 < n_tot:
                ln1(c + 1, tok_dma_next)

            if not warm:
                for hh in range(2):
                    sl = slice(hh * 512, (hh + 1) * 512)
                    for it in range(KT):
                        E("tensor", "matmul", pA[:, sl],
                          qf[:, it * 128:(it + 1) * 128],
                          S_sb[:, it * D + hh * 512: it * D + hh * 512 + 512],
                          start=(it == 0), stop=False,
                          deps=[tok["v14"], tok["v20last"], tok["v10"]]
                          if (hh == 0 and it == 0) else (), inc=0)
                    for ph, (lh, rh) in enumerate(
                            ((pmh, vh16), (pml, vh16), (pmh, vl16))):
                        tok_t = E("tensor", "matmul", pA[:, sl], lh[:], rh[:, sl],
                                  start=False, stop=(ph == 2),
                                  deps=[tok["v16"], tok["v12"]]
                                  if (hh == 0 and ph == 0) else (),
                                  inc=1 if (hh == 1 and ph == 2) else 0)
                tok["num"] = tok_t
                for it in range(KT):
                    E("tensor", "matmul", pB[:, 512:513],
                      qf[:, it * 128:(it + 1) * 128], z_sb[:, it:it + 1],
                      start=(it == 0), stop=False,
                      deps=[tok["v21"], tok["v17"], tok["v15"]] if it == 0 else (),
                      inc=0)
                E("tensor", "matmul", pB[:, 512:513], pmh[:], ones16[:],
                  start=False, stop=False)
                tok["den"] = E("tensor", "matmul", pB[:, 512:513], pml[:],
                               ones16[:], start=False, stop=True, inc=1)
                tok["v17"] = E("vector", "tensor_scalar", out=r2[:, 0:1],
                               in0=pB[:, 512:513], scalar1=EPS_DEN, scalar2=None,
                               op0=ALU.add, deps=[tok["den"]])
                E("vector", "reciprocal", out=r2[:, 1:2], in_=r2[:, 0:1])
                tok["v19"] = E("vector", "scalar_tensor_tensor", out=xin[g][:],
                               in0=pA[:], scalar=r2[:, 1:2], in1=xin[g][:],
                               op0=ALU.mult, op1=ALU.add,
                               deps=[tok["num"], tok["s5"], tok["x2dma"]])
                tok["x2dma"] = E("gdma", "dma_start",
                                 out=x2s_d[m * 128:(m + 1) * 128, :],
                                 in_=xin[g][:], deps=[tok["v19"]])

            wq_sub(0)
            if c + 1 < n_tot:
                front(c + 1)
            wq_sub(1)

            for it in range(KT):
                pdst = pA if it % 2 == 0 else pC
                if it == 0:
                    dfree = [tok["v13"], tok["v12"], tok["xq_ev"],
                             (tok["v19"] if not warm else None)]
                elif it == 1:
                    dfree = [tok["xq_odd"]]
                else:
                    dfree = [tok["v20"][it - 2]]
                for hh in range(2):
                    sl = slice(hh * 512, (hh + 1) * 512)
                    ksl = slice(it * 128, (it + 1) * 128)
                    for ph, (lh, rh) in enumerate(
                            ((kth16, vh16), (ktl16, vh16), (kth16, vl16))):
                        tok_t = E("tensor", "matmul", pdst[:, sl],
                                  lh[:, ksl], rh[:, sl],
                                  start=(ph == 0), stop=(ph == 2),
                                  deps=dfree if (hh == 0 and ph == 0) else (),
                                  inc=1 if (hh == 1 and ph == 2) else 0)
                tok["L"][it] = tok_t
                tok["v20"][it] = E("vector", "scalar_tensor_tensor",
                                   out=S_sb[:, it * D:(it + 1) * D],
                                   in0=S_sb[:, it * D:(it + 1) * D],
                                   scalar=gcvec[:, 0:1], in1=pdst[:],
                                   op0=ALU.mult, op1=ALU.add,
                                   deps=[tok["L"][it], tok["num"]])
            tok["v20last"] = tok["v20"][KT - 1]
            tok["xq_ev"] = tok["v20"][KT - 2]
            tok["xq_odd"] = tok["v20"][KT - 1]

            wq_sub(2)

            for it in range(KT):
                ksl = slice(it * 128, (it + 1) * 128)
                for ph, lh in enumerate((kth16, ktl16)):
                    tok_t = E("tensor", "matmul", pB[:, 520 + it:521 + it],
                              lh[:, ksl], ones16[:],
                              start=(it == 0 and ph == 0),
                              stop=(it == KT - 1 and ph == 1),
                              deps=[tok["v17"], tok["v13"], tok["v11"]]
                              if (it == 0 and ph == 0) else (),
                              inc=1 if (it == KT - 1 and ph == 1) else 0)
            tok["z"] = tok_t
            tok["v21"] = E("vector", "scalar_tensor_tensor", out=z_sb[:],
                           in0=z_sb[:], scalar=gcvec[:, 0:1], in1=pB[:, 520:528],
                           op0=ALU.mult, op1=ALU.add,
                           deps=[tok["z"], tok["den"]])
            wq_sub(3)

            if warm and c == n_warm - 1:
                E("vector", "tensor_scalar", out=S_sb[:], in0=S_sb[:],
                  scalar1=maskvec[:, 0:1], scalar2=None, op0=ALU.mult,
                  deps=[tok["v20last"], t_in1])
                tok["v21"] = E("vector", "tensor_scalar", out=z_sb[:], in0=z_sb[:],
                               scalar1=maskvec[:, 0:1], scalar2=None, op0=ALU.mult)
                tok["v20last"] = tok["v21"]
                tok["xq_ev"] = tok["v21"]
                tok["xq_odd"] = tok["v21"]



        def w_subtiles(which):
            if which == 1:
                nj = H // 1024
                for i in range(KT):
                    for j in range(nj):
                        yield (w1t_d[i * 128:(i + 1) * 128, j * 1024:(j + 1) * 1024],
                               i * nj + j)
            else:
                for i in range(H // 128):
                    yield w2t_d[i * 128:(i + 1) * 128, :], i

        n1 = KT * (H // 1024)
        n2 = H // 128
        subs = ([(ap, 0, idx) for ap, idx in w_subtiles(1)]
                + [(ap, n1, idx) for ap, idx in w_subtiles(2)])
        nsub = len(subs)
        wq_state = {"rd": [wsplit["rd"][0], wsplit["rd"][1]], "n": 0, "w8dma": None,
                    "wg_n": 0, "w8": [None, None],
                    "wsb": None, "abslast": None, "mid": (None, ())}

        def wq_pass1_slice(items):
            for ap, base, idx in items:
                b = wq_state["n"] % 2
                wq_state["n"] += 1
                buf = wstg if b == 0 else qt
                td = E("sync", "dma_start", out=buf[:],
                       in_=ap, deps=[wq_state["rd"][b]])
                t = E("scalar", "activation", out=dmp2[:], in_=buf[:],
                      func=AF.Abs, bias=0.0, scale=1.0,
                      accum_out=absacc[:, base + idx:base + idx + 1], deps=[td])
                wq_state["rd"][b] = t
                wq_state["abslast"] = t

        def wq_ws_compute():
            E("vector", "tensor_reduce", out=wqr[:, 0:1], in_=absacc[:, 0:n1],
              axis=AX.X, op=ALU.add, deps=[wq_state["abslast"]])
            tred = E("vector", "tensor_reduce", out=wqr[:, 1:2],
                     in_=absacc[:, n1:n1 + n2], axis=AX.X, op=ALU.add)
            tsum = E("tensor", "matmul", pD[0:1, 256:258], onescol[:],
                     wqr[:, 0:2], start=True, stop=True,
                     deps=[tok["pDfree"], tred], inc=1)
            tgc = E("vector", "tensor_copy", out=wqr[0:1, 2:4],
                    in_=pD[0:1, 256:258], deps=[tsum])
            E("vector", "tensor_scalar", out=wqr[0:1, 2:3], in0=wqr[0:1, 2:3],
              scalar1=1.0 / (D * H), scalar2=EPS_Q, op0=ALU.mult, op1=ALU.max,
              deps=[tgc])
            tv = E("vector", "tensor_scalar", out=wqr[0:1, 3:4], in0=wqr[0:1, 3:4],
                   scalar1=1.0 / (D * H), scalar2=EPS_Q, op0=ALU.mult, op1=ALU.max)
            E("tensor", "matmul", pD[:, 0:1], onesrow[:], wqr[0:1, 2:3],
              start=True, stop=False, deps=[tok["pDfree"], tv])
            tbc = E("tensor", "matmul", pD[:, 128:129], onesrow[:], wqr[0:1, 3:4],
                    start=False, stop=True, inc=1)
            E("vector", "tensor_copy", out=ws1b[:], in_=pD[:, 0:1], deps=[tbc])
            E("vector", "tensor_copy", out=ws2b[:], in_=pD[:, 128:129])
            E("vector", "tensor_scalar", out=hws1b[:, 0:1], in0=ws1b[:],
              scalar1=0.5, scalar2=None, op0=ALU.mult)
            E("vector", "tensor_scalar", out=hws1b[:, 1:2], in0=hws1b[:, 0:1],
              scalar1=-1.0, scalar2=None, op0=ALU.mult)
            E("vector", "tensor_scalar", out=hws2b[:, 0:1], in0=ws2b[:],
              scalar1=0.5, scalar2=None, op0=ALU.mult)
            E("vector", "tensor_scalar", out=hws2b[:, 1:2], in0=hws2b[:, 0:1],
              scalar1=-1.0, scalar2=None, op0=ALU.mult)
            tv2 = E("vector", "tensor_scalar", out=w2s127[:], in0=ws2b[:],
                    scalar1=1.0 / QB, scalar2=None, op0=ALU.mult)
            wq_state["wsb"] = tv2
            tok["pDfree"] = tv2

        def wq_pass2_slice(items):
            for ap, base, idx in items:
                which = 1 if base == 0 else 2
                hb = hws1b if which == 1 else hws2b
                b = wq_state["n"] % 2
                wq_state["n"] += 1
                buf = wstg if b == 0 else qt
                td = E("sync", "dma_start", out=buf[:],
                       in_=ap, deps=[wq_state["rd"][b]])
                for hf in range(2):
                    bsl = slice(hf * 512, (hf + 1) * 512)
                    p8 = wq_state["wg_n"] % 2
                    wq_state["wg_n"] += 1
                    bge = wg1 if p8 == 0 else wg2
                    ble = wg2 if p8 == 0 else wg1
                    tg1 = E("vector", "tensor_scalar", out=bge[:], in0=buf[:, bsl],
                            scalar1=hb[:, 0:1], scalar2=None, op0=ALU.is_ge,
                            deps=[td, wq_state["wsb"], wq_state["w8"][p8]])
                    tg2 = E("vector", "tensor_scalar", out=ble[:], in0=buf[:, bsl],
                            scalar1=hb[:, 1:2], scalar2=None, op0=ALU.is_le,
                            deps=[wq_state["w8"][1 - p8]])
                    wq_state["rd"][b] = tg2
                    tv = E("vector", "tensor_tensor", out=bge[:], in0=bge[:],
                           in1=ble[:], op=ALU.subtract, deps=[tg1, tg2])
                    if which == 1:
                        nj = H // 1024
                        i, j = idx // nj, idx % nj
                        dst = w1q_d[i * 128:(i + 1) * 128,
                                    j * 1024 + hf * 512:j * 1024 + (hf + 1) * 512]
                    else:
                        dst = w2q_d[idx * 128:(idx + 1) * 128, bsl]
                    wq_state["w8"][p8] = E("gdma", "dma_start", out=dst,
                                           in_=bge[:], deps=[tv])
                    wq_state["w8dma"] = wq_state["w8"][p8]

        p1e = max(1, min(16, n_tot // 2))
        cnts = [0] * n_tot
        nw = min(n_warm, p1e)
        rem = nsub
        if p1e - nw >= 4:
            for c in range(nw):
                cnts[c] = min(rem, 2)
                rem -= cnts[c]
        m1 = list(range(nw, p1e))
        for i, c in enumerate(m1):
            t = -(-rem // (len(m1) - i))
            cnts[c] = t
            rem -= t
        if rem:
            cnts[p1e - 1] += rem
        rem = nsub
        m2 = list(range(p1e, n_tot))
        for i, c in enumerate(m2):
            t = -(-rem // (len(m2) - i))
            cnts[c] = t
            rem -= t

        ln1(0, t_x0)
        front(0)
        o1 = o2 = 0
        for c in range(n_tot):
            if c < p1e:
                items = subs[o1:o1 + cnts[c]]
                o1 += cnts[c]
                kindw = "p1"
            else:
                items = subs[o2:o2 + cnts[c]]
                o2 += cnts[c]
                kindw = "p2"
            q4 = -(-len(items) // 4) if items else 1
            wq_state["mid"] = (kindw if items else None,
                               [items[i:i + q4] for i in range(0, len(items), q4)])
            chunk(c)
            if c >= n_warm:
                ln2_stats(c)
            if c == p1e - 1:
                wq_state["ws_pending"] = True

        tok_wq = wq_state["w8dma"]
        barrA = E.barrier_tokens()
        esA.close()

        esB = ExitStack()
        sbB = lambda name, shape, dt=F32: esB.enter_context(nc.sbuf_tensor(name, shape, dt))
        W1Q = sbB("W1Q", [128, KT * H], FP8)
        W2Q = sbB("W2Q", [128, (H // 128) * D], FP8)
        b1b = sbB("b1b", [128, H])
        b2b = sbB("b2b", [128, D])
        x2b = [sbB("x2b0", [128, D]), sbB("x2b1", [128, D]), sbB("x2b2", [128, D])]
        vq = [sbB("vq0", [128, D]), sbB("vq1", [128, D])]
        tq = [esB.enter_context(nc.sbuf_tensor("tq0", [128, D], I32)),
              esB.enter_context(nc.sbuf_tensor("tq1", [128, D], I32))]
        cbuf = sbB("cbuf", [128, D])
        cbuf2 = sbB("cbuf2", [128, D])
        xq_tm = sbB("xq_tm", [128, D], BF16)
        xqT = sbB("xqT", [128, D], BF16)
        hdeq = [sbB("hdeq0", [128, H]), sbB("hdeq1", [128, H])]
        hq_tm = sbB("hq_tm", [128, H], BF16)
        hqT = sbB("hqT", [128, H], BF16)
        outb = [sbB("outb0", [128, D]), sbB("outb1", [128, D])]
        pH = [esB.enter_context(nc.psum_tensor("pH0", [128, 512], F32)),
              esB.enter_context(nc.psum_tensor("pH1", [128, 512], F32))]
        pO = esB.enter_context(nc.psum_tensor("pO", [128, 1024], F32))
        pT1 = esB.enter_context(nc.psum_tensor("pT1", [128, 2048], BF16))
        pT2 = esB.enter_context(nc.psum_tensor("pT2", [128, 2048], BF16))
        b1row = sbB("b1rowB", [1, H])
        b2row = sbB("b2rowB", [1, D])

        E("sync", "dma_start", out=b1row[:], in_=b1_d[:], deps=barrA)
        t_rowsB = E("sync", "dma_start", out=b2row[:], in_=b2_d[:])
        t_x2b0 = E("sync", "dma_start", out=x2b[0][:], in_=x2s_d[0:128, :])
        w1l, w2l = [], []
        for kt in range(KT):
            w1l.append(E("sync" if kt % 2 == 0 else "gdma", "dma_start",
                         out=W1Q[:, kt * H:(kt + 1) * H],
                         in_=w1q_d[kt * 128:(kt + 1) * 128, :], deps=barrA))
        for kt in range(H // 128):
            w2l.append(E("sync" if kt % 2 == 0 else "gdma", "dma_start",
                         out=W2Q[:, kt * D:(kt + 1) * D],
                         in_=w2q_d[kt * 128:(kt + 1) * 128, :], deps=barrA))

        tprev = None
        for hh in range(KT):
            sl = slice(hh * 512, (hh + 1) * 512)
            tt_b1 = E("tensor", "matmul", pH[hh % 2][:], onesrow[:], b1row[:, sl],
                      start=True, stop=True,
                      deps=barrA + [t_rowsB] + ([tprev] if tprev else []), inc=1)
            tprev = E("vector", "tensor_copy", out=b1b[:, sl], in_=pH[hh % 2][:],
                      deps=[tt_b1])
        for hh in range(2):
            sl = slice(hh * 512, (hh + 1) * 512)
            tt_b2 = E("tensor", "matmul", pO[:, sl], onesrow[:], b2row[:, sl],
                      start=True, stop=True, inc=1)
        tv_b2 = E("vector", "tensor_copy", out=b2b[:], in_=pO[:], deps=[tt_b2])

        E.strict_scalar = False
        tokB = {
            "trx": None, "trh_t": [None, None],
            "s_xqT": tprev, "s_hqT1": tprev, "s_hqT2": tprev,
            "dq": [tprev, tprev], "epi": [tv_b2, tv_b2],
            "x2rd": [None, None, None], "ydma": [None, None],
            "x2w": [t_x2b0, None, None],
            "h1last": None, "o": tv_b2, "tv_hq": None,
            "chain_last": [None, None],
            "chain_s1": [None, None], "chain_s2": [None, None],
            "rqh": None, "hql01": None, "lead_last": None, "chain_tsc": None,
        }

        def quant_chain(par, src_ap, sca, scb, n, out_bf, war_dep, deps0=(),
                        leads=None):
            if leads is not None:
                tv0, tsc = leads
            else:
                d0 = list(deps0)
                tv0 = E("scalar", "activation", out=vq[par][:, 0:n], in_=src_ap,
                        func=AF.Identity, scale=sca,
                        bias=(scb if scb is not None else 0.0),
                        deps=d0 + [tokB["chain_s1"][par]])
                tsc = E("scalar", "activation", out=tq[par][:, 0:n], in_=src_ap,
                        func=AF.Identity, scale=sca,
                        bias=(scb if scb is not None else 0.0),
                        deps=d0 + [tokB["chain_s2"][par]])
                tokB["chain_tsc"] = tsc
            E("vector", "scalar_tensor_tensor", out=cbuf[:, 0:n],
              in0=tq[par][:, 0:n], scalar=0.5, in1=vq[par][:, 0:n],
              op0=ALU.add, op1=ALU.is_le, deps=[tv0, tsc])
            tokB["chain_s1"][par] = E("vector", "scalar_tensor_tensor",
                                      out=cbuf2[:, 0:n],
                                      in0=tq[par][:, 0:n], scalar=-0.5,
                                      in1=vq[par][:, 0:n],
                                      op0=ALU.add, op1=ALU.is_ge)
            tokB["chain_s2"][par] = E("vector", "scalar_tensor_tensor",
                                      out=cbuf[:, 0:n],
                                      in0=cbuf[:, 0:n], scalar=1.0,
                                      in1=tq[par][:, 0:n],
                                      op0=ALU.mult, op1=ALU.add)
            t = E("vector", "tensor_tensor", out=out_bf, in0=cbuf[:, 0:n],
                  in1=cbuf2[:, 0:n], op=ALU.subtract, deps=[war_dep])
            tokB["chain_last"][par] = t
            return t

        def emit_prefetch(mq):
            if 0 < mq < n_main:
                g3 = mq % 3
                tokB["x2w"][g3] = E("sync", "dma_start", out=x2b[g3][:],
                                    in_=x2s_d[mq * 128:(mq + 1) * 128, :],
                                    deps=[tokB["x2rd"][g3]])


        def emit_hq_leads(mh):
            if 0 <= mh < n_main:
                p = mh % 2
                tokB["hql01"] = []
                for sub in range(2):
                    par = (sub + 1) % 2
                    sl = slice(sub * 1024, (sub + 1) * 1024)
                    tv0 = E("scalar", "activation", out=vq[par][:],
                            in_=hdeq[p][:, sl], func=AF.Identity,
                            scale=r2[:, 2:3], bias=0.0,
                            deps=[tokB["rqh"], tokB["chain_s1"][par]])
                    tsc = E("scalar", "activation", out=tq[par][:],
                            in_=hdeq[p][:, sl], func=AF.Identity,
                            scale=r2[:, 2:3], bias=0.0,
                            deps=[tokB["rqh"], tokB["chain_s2"][par]])
                    tokB["hql01"].append((tv0, tsc))

        def emit_chains(mf, mh):
            if 0 <= mh < n_main:
                p = mh % 2
                for sub in range(4):
                    sl = slice(sub * 1024, (sub + 1) * 1024)
                    tokB["tv_hq"] = quant_chain(
                        (sub + 1) % 2, hdeq[p][:, sl], r2[:, 2:3], None, 1024,
                        hq_tm[:, sl], tokB["trh_t"][0 if sub < 2 else 1],
                        deps0=[tokB["rqh"]],
                        leads=tokB["hql01"][sub] if sub < 2 else None)
                    if sub >= 2:
                        tokB["lead_last"] = tokB["chain_tsc"]

        def emit_front(mf):
            if not (0 <= mf < n_main):
                return
            pf = mf % 2
            rpartf = r2 if pf == 0 else r1
            sdq1c = r2[:, 0:1] if pf == 0 else r2[:, 14:15]
            for dt_ in range(KT):
                tok_t = E("tensor", "transpose",
                          pT2[:, dt_ * 128:(dt_ + 1) * 128],
                          xq_tm[:, dt_ * 128:(dt_ + 1) * 128], idenb[:],
                          deps=[tokB["xqw"], tokB["s_hqT2"]] if dt_ == 0 else (),
                          inc=1 if dt_ == KT - 1 else 0)
            tokB["trx"] = tok_t
            if mf + 1 < n_main:
                tokB["xqw"] = E("gdma", "dma_start", out=xq_tm[:],
                                in_=xqs_d[(mf + 1) * 128:(mf + 2) * 128, :],
                                deps=[tok_t])
            tokB["s_xqT"] = E("scalar", "copy", out=xqT[:], in_=pT2[:, 0:1024],
                              deps=[tok_t, tokB["h1last"]])
            tv_s1 = E("vector", "tensor_scalar", out=sdq1c,
                      in0=stats[:, 3 * mf + 2:3 * mf + 3],
                      scalar1=ws1b[:, 0:1], scalar2=None, op0=ALU.mult)
            tgel = {}
            def part_max(hh):
                E("vector", "tensor_reduce", out=rpartf[:, 5 + hh:6 + hh],
                  in_=hdeq[pf][:, hh * 512:(hh + 1) * 512], axis=AX.X,
                  op=ALU.max, apply_absolute_value=True, deps=[tgel[hh]])
            tg = None
            for hh in range(KT):
                pg = hh % 2
                for dt_ in range(KT):
                    d_ = ([tokB["s_xqT"], tokB["dq"][pg]] if dt_ == 0 else [])
                    if mf == 0 and hh == 0:
                        d_ = d_ + [w1l[dt_]]
                    tok_t = E("tensor", "matmul", pH[pg][:],
                              xqT[:, dt_ * 128:(dt_ + 1) * 128],
                              W1Q[:, dt_ * H + hh * 512: dt_ * H + hh * 512 + 512],
                              start=(dt_ == 0), stop=(dt_ == KT - 1),
                              deps=d_,
                              inc=1 if dt_ == KT - 1 else 0)
                sl = slice(hh * 512, (hh + 1) * 512)
                tokB["dq"][pg] = E("vector", "scalar_tensor_tensor",
                                   out=hdeq[pf][:, sl], in0=pH[pg][:],
                                   scalar=sdq1c, in1=b1b[:, sl],
                                   op0=ALU.mult, op1=ALU.add,
                                   deps=[tok_t, tv_s1])
                tg = E("scalar", "activation", out=hdeq[pf][:, sl],
                       in_=hdeq[pf][:, sl], func=AF.Gelu, bias=0.0,
                       scale=1.0, deps=[tokB["dq"][pg], tg])
                tgel[hh] = tg
                if hh >= 1:
                    part_max(hh - 1)
            part_max(KT - 1)
            tokB["h1last"] = tok_t
            E("vector", "tensor_reduce", out=r2[:, 1:2], in_=rpartf[:, 5:13],
              axis=AX.X, op=ALU.max)
            E("vector", "tensor_scalar", out=r2[:, 1:2], in0=r2[:, 1:2],
              scalar1=EPS_Q, scalar2=None, op0=ALU.max)
            E("vector", "reciprocal", out=r2[:, 2:3], in_=r2[:, 1:2],
              deps=[tokB["lead_last"]])
            tokB["rqh"] = E("vector", "tensor_scalar", out=r2[:, 2:3],
                            in0=r2[:, 2:3], scalar1=QB, scalar2=None,
                            op0=ALU.mult)
            E("vector", "tensor_scalar", out=r2[:, 3 + pf:4 + pf],
              in0=r2[:, 1:2], scalar1=w2s127[:, 0:1], scalar2=None,
              op0=ALU.mult)

        tokB["xqw"] = E("gdma", "dma_start", out=xq_tm[:], in_=xqs_d[0:128, :],
                        deps=barrA)
        emit_chains(0, -1)
        emit_front(0)
        emit_prefetch(1)

        for m in range(n_main + 1):
            emit_hq_leads(m)
            if m >= 1:
                for grp in range(2):
                    pdst = pT1 if grp == 0 else pT2
                    dep0 = ([tokB["tv_hq"], tokB["s_hqT1"]] if grp == 0
                            else [tokB["tv_hq"], tokB["s_xqT"]])
                    for j in range(16):
                        dt_ = grp * 16 + j
                        tok_t = E("tensor", "transpose",
                                  pdst[:, j * 128:(j + 1) * 128],
                                  hq_tm[:, dt_ * 128:(dt_ + 1) * 128], idenb[:],
                                  deps=dep0 if j == 0 else (),
                                  inc=1 if j == 15 else 0)
                    tokB["trh_t"][grp] = tok_t
                    if grp == 0:
                        tokB["s_hqT1"] = E("scalar", "copy", out=hqT[:, 0:2048],
                                           in_=pT1[:], deps=[tok_t, tokB["o"]])
                    else:
                        tokB["s_hqT2"] = E("scalar", "copy", out=hqT[:, 2048:4096],
                                           in_=pT2[:], deps=[tok_t, tokB["o"]])

            if m >= 1:
                mp = m - 1
                gp = mp % 3
                gpy = mp % 2
                for hh in range(2):
                    sl = slice(hh * 512, (hh + 1) * 512)
                    for kk in range(H // 128):
                        d_ = ([tokB["s_hqT1"], tokB["s_hqT2"],
                               tokB["epi"][hh]] if kk == 0 else [])
                        if mp == 0 and hh == 0:
                            d_ = d_ + [w2l[kk]]
                        tok_t = E("tensor", "matmul", pO[:, sl],
                                  hqT[:, kk * 128:(kk + 1) * 128],
                                  W2Q[:, kk * D + hh * 512: kk * D + hh * 512 + 512],
                                  start=(kk == 0), stop=(kk == H // 128 - 1),
                                  deps=d_,
                                  inc=1 if kk == H // 128 - 1 else 0)
                tokB["o"] = tok_t
            emit_chains(m + 1, m)
            if m >= 1:
                for hh in range(2):
                    sl = slice(hh * 512, (hh + 1) * 512)
                    E("vector", "scalar_tensor_tensor", out=outb[gpy][:, sl],
                      in0=pO[:, sl], scalar=r2[:, 3 + gpy:4 + gpy],
                      in1=x2b[gp][:, sl], op0=ALU.mult, op1=ALU.add,
                      deps=[tokB["o"], tokB["ydma"][gpy], tokB["x2w"][gp]])
                    tokB["epi"][hh] = E("vector", "tensor_tensor",
                                        out=outb[gpy][:, sl], in0=outb[gpy][:, sl],
                                        in1=b2b[:, sl], op=ALU.add)
                tokB["x2rd"][gp] = tokB["epi"][1]
                tokB["ydma"][gpy] = E("sync", "dma_start",
                                      out=y_d[mp * 128:(mp + 1) * 128, :],
                                      in_=outb[gpy][:], deps=[tokB["epi"][1]])
            emit_prefetch(m + 2)
            emit_front(m + 1)

        esB.close()
        E.final_wait("sync")
        E.flush(block)
    return nc


def shard_inputs(x, Wq, bq, Wk, bk, Wv, bv, gamma, W1, b1, W2, b2,
                 n_span=4, n_warm=4):
    f32 = np.float32
    B, S, D_ = x.shape
    span = S // n_span
    W = n_warm * C
    x = np.ascontiguousarray(np.asarray(x, dtype=f32))
    common = {
        "wqt": np.ascontiguousarray(np.asarray(Wq, f32).T),
        "wkt": np.ascontiguousarray(np.asarray(Wk, f32).T),
        "wvt": np.ascontiguousarray(np.asarray(Wv, f32).T),
        "bq": np.asarray(bq, f32).reshape(1, -1),
        "bk": np.asarray(bk, f32).reshape(1, -1),
        "bv": np.asarray(bv, f32).reshape(1, -1),
        "gam": np.asarray(gamma, f32).reshape(1, 1),
        "w1t": np.ascontiguousarray(np.asarray(W1, f32).T),
        "b1": np.asarray(b1, f32).reshape(1, -1),
        "w2t": np.ascontiguousarray(np.asarray(W2, f32).T),
        "b2": np.asarray(b2, f32).reshape(1, -1),
    }
    in_maps = []
    for core in range(B * n_span):
        b = core // n_span
        s = core % n_span
        if s == 0:
            xs = np.concatenate([np.zeros((W, D_), f32), x[b, 0:span]], axis=0)
            wm = np.zeros((128, 1), f32)
        else:
            xs = x[b, s * span - W:(s + 1) * span]
            wm = np.ones((128, 1), f32)
        mm = dict(common)
        mm["x"] = np.ascontiguousarray(xs)
        mm["wmask"] = wm
        in_maps.append(mm)
    return in_maps, span


def kernel(L0_attn_x_seq, L0_attn_Wq, L0_attn_bq, L0_attn_Wk, L0_attn_bk,
           L0_attn_Wv, L0_attn_bv, L0_attn_gamma, L0_ffn_W1, L0_ffn_b1,
           L0_ffn_W2, L0_ffn_b2):
    n_span, n_warm = 4, 3
    B, S, D_ = L0_attn_x_seq.shape
    in_maps, span = shard_inputs(
        L0_attn_x_seq, L0_attn_Wq, L0_attn_bq, L0_attn_Wk, L0_attn_bk,
        L0_attn_Wv, L0_attn_bv, L0_attn_gamma, L0_ffn_W1, L0_ffn_b1,
        L0_ffn_W2, L0_ffn_b2, n_span=n_span, n_warm=n_warm)
    nc = build_nc(n_main=span // C, n_warm=n_warm)
    res = run_bass_kernel_spmd(nc, in_maps, core_ids=list(range(8)))
    out = np.empty((B, S, D_), dtype=np.float32)
    for core in range(8):
        b = core // n_span
        s = core % n_span
        out[b, s * span:(s + 1) * span] = res.results[core]["y"]
    return out


# version 36
# speedup vs baseline: 1.0880x; 1.0880x over previous
import numpy as np
from contextlib import ExitStack

import concourse.bass as bass
import concourse.mybir as mybir
from concourse.bass_utils import run_bass_kernel_spmd

F32 = mybir.dt.float32
F16 = mybir.dt.float16
BF16 = mybir.dt.bfloat16
FP8 = mybir.dt.float8e4
I32 = mybir.dt.int32
AF = mybir.ActivationFunctionType
ALU = mybir.AluOpType
AX = mybir.AxisListType

D = 1024
H = 4096
C = 128
KT = D // 128
EPS_LN = 1e-5
EPS_DEN = 1e-6
EPS_Q = 1e-5
QB = 127.0

ENGS = ("tensor", "vector", "scalar", "gpsimd", "sync")
SEMS = ENGS + ("gdma", "sdma")


class Em:

    def __init__(self, nc, es):
        self.nc = nc
        self.sem = {n: es.enter_context(nc.semaphore("S_" + n)) for n in SEMS}
        self.cnt = {n: 0 for n in SEMS}
        self.seen = {a: {b: 0 for b in SEMS} for a in SEMS}
        self.ops = {n: [] for n in ENGS}
        self.strict_scalar = True

    def __call__(self, eng, meth, *args, deps=(), inc=None, **kw):
        sem = eng
        if eng == "gdma":
            eng = "gpsimd"
        elif eng == "sdma":
            eng = "scalar"
        dl = [d for d in deps if d is not None]
        if sem in ("vector", "scalar", "gpsimd") and self.cnt[sem] > 0:
            dl.append((sem, self.cnt[sem]))
        if sem in ("sync", "gdma", "sdma") and self.cnt[sem] > 0:
            dl.append((sem, self.cnt[sem]))
        if (self.strict_scalar and sem in ("scalar", "gpsimd")
                and self.cnt["vector"] > 0):
            dl.append(("vector", self.cnt["vector"]))
        waits = []
        for dn, dv in dl:
            if dv > self.seen[eng][dn]:
                waits.append((dn, dv))
                self.seen[eng][dn] = dv
        if inc is None:
            inc = 16 if meth.startswith("dma") else (1 if eng != "tensor" else 0)
        self.ops[eng].append((meth, args, kw, waits, inc, sem))
        if inc:
            self.cnt[sem] += inc
            return (sem, self.cnt[sem])
        return None

    def barrier_tokens(self):
        return [(n, self.cnt[n]) for n in SEMS]

    def final_wait(self, eng):
        self.ops[eng].append(("__wait__", (), {},
                              [(n, self.cnt[n]) for n in
                               ("sync", "gdma", "sdma")], 0, eng))

    def flush(self, block):
        for name in ENGS:
            ops = self.ops[name]

            def mk(ops=ops):
                def run(h):
                    for meth, args, kw, waits, inc, sem in ops:
                        for dn, dv in waits:
                            h.wait_ge(self.sem[dn], dv)
                        if meth == "__wait__":
                            continue
                        r = getattr(h, meth)(*args, **kw)
                        if inc:
                            r.then_inc(self.sem[sem], inc)
                return run
            getattr(block, name)(mk())


def build_nc(n_main=32, n_warm=4):
    nc = bass.Bass("TRN2", target_bir_lowering=False)
    n_tot = n_main + n_warm
    n_tok_in = n_tot * C
    n_tok_out = n_main * C

    x_d = nc.dram_tensor("x", [n_tok_in, D], F32, kind="ExternalInput").ap()
    wqt_d = nc.dram_tensor("wqt", [D, D], F32, kind="ExternalInput").ap()
    wkt_d = nc.dram_tensor("wkt", [D, D], F32, kind="ExternalInput").ap()
    wvt_d = nc.dram_tensor("wvt", [D, D], F32, kind="ExternalInput").ap()
    bq_d = nc.dram_tensor("bq", [1, D], F32, kind="ExternalInput").ap()
    bk_d = nc.dram_tensor("bk", [1, D], F32, kind="ExternalInput").ap()
    bv_d = nc.dram_tensor("bv", [1, D], F32, kind="ExternalInput").ap()
    gam_d = nc.dram_tensor("gam", [1, 1], F32, kind="ExternalInput").ap()
    wmask_d = nc.dram_tensor("wmask", [128, 1], F32, kind="ExternalInput").ap()
    w1t_d = nc.dram_tensor("w1t", [D, H], F32, kind="ExternalInput").ap()
    b1_d = nc.dram_tensor("b1", [1, H], F32, kind="ExternalInput").ap()
    w2t_d = nc.dram_tensor("w2t", [H, D], F32, kind="ExternalInput").ap()
    b2_d = nc.dram_tensor("b2", [1, D], F32, kind="ExternalInput").ap()
    y_d = nc.dram_tensor("y", [n_tok_out, D], F32, kind="ExternalOutput").ap()
    x2s_d = nc.dram_tensor("x2scr", [n_tok_out, D], F32).ap()
    xqs_d = nc.dram_tensor("xqscr", [n_tok_out, D], BF16).ap()
    w1q_d = nc.dram_tensor("w1q", [D, H], FP8).ap()
    w2q_d = nc.dram_tensor("w2q", [H, D], FP8).ap()

    es = ExitStack()
    with es:
        block = es.enter_context(nc.Block())
        E = Em(nc, es)

        psb = lambda name, shape, dt=F32: es.enter_context(nc.sbuf_tensor(name, shape, dt))
        onesrow = psb("onesrow", [1, 128])
        onescol = psb("onescol", [128, 1])
        gvec = psb("gvec", [128, 1])
        givec = psb("givec", [128, 1])
        gcvec = psb("gcvec", [128, 1])
        maskvec = psb("maskvec", [128, 1])
        ws1b = psb("ws1b", [128, 1])
        hws1b = psb("hws1b", [128, 2])
        ws2b = psb("ws2b", [128, 1])
        hws2b = psb("hws2b", [128, 2])
        w2s127 = psb("w2s127", [128, 1])
        iden = psb("iden", [128, 128])
        idenb = psb("idenb", [128, 128], BF16)
        maskUT = psb("maskUT", [128, 128])
        gamsc = psb("gamsc", [1, 1])
        epsv = psb("epsv", [128, 1])
        stats = psb("stats", [128, 3 * n_main])
        r1 = psb("r1", [128, 16])
        r2 = psb("r2", [128, 16])

        esA = ExitStack()
        pA = esA.enter_context(nc.psum_tensor("pA", [128, 1024], F32))
        pB = esA.enter_context(nc.psum_tensor("pB", [128, 1024], F32))
        pC = esA.enter_context(nc.psum_tensor("pC", [128, 1024], F32))
        pD = esA.enter_context(nc.psum_tensor("pD", [128, 1024], F32))

        gbq = esA.enter_context(nc.sbuf_tensor("gbq", [128, D], F32))
        gbk = esA.enter_context(nc.sbuf_tensor("gbk", [128, D], F32))
        bvb = esA.enter_context(nc.sbuf_tensor("bvb", [128, D], F32))

        esI = ExitStack()
        isb = lambda name, shape, dt=F32: esI.enter_context(nc.sbuf_tensor(name, shape, dt))
        gamrow = isb("gamrow", [1, 128])
        zerrow = isb("zerrow", [1, 128])
        grow = isb("grow", [1, 128])
        girow = isb("girow", [1, 128])
        iotat = isb("iotat", [128, 128])
        bqrow = isb("bqrow", [1, D])
        bkrow = isb("bkrow", [1, D])
        bvrow = isb("bvrow", [1, D])

        t_in0 = E("sync", "dma_start", out=gamsc[:], in_=gam_d[:])
        E("sync", "dma_start", out=bqrow[:], in_=bq_d[:])
        E("sync", "dma_start", out=bkrow[:], in_=bk_d[:])
        t_rows = E("sync", "dma_start", out=bvrow[:], in_=bv_d[:])
        t_in1 = E("sync", "dma_start", out=maskvec[:], in_=wmask_d[:])

        tg_io = E("gpsimd", "iota", out=iotat[:], pattern=[[1, 128]], base=0,
                  channel_multiplier=-1, allow_small_or_imprecise_dtypes=True)

        E("vector", "memset", onesrow[:], 1.0)
        E("vector", "memset", onescol[:], 1.0)
        E("vector", "memset", zerrow[:], 0.0)
        E("vector", "memset", epsv[:], EPS_LN)
        E("vector", "tensor_scalar", out=maskUT[:], in0=iotat[:], scalar1=0.0,
          scalar2=None, op0=ALU.is_ge, deps=[tg_io])
        E("vector", "tensor_scalar", out=iden[:], in0=iotat[:], scalar1=0.0,
          scalar2=None, op0=ALU.is_equal)
        E("vector", "tensor_copy", out=idenb[:], in_=iden[:])
        E("vector", "memset", gamrow[:], 1.0)
        E("vector", "tensor_scalar", out=gamrow[:], in0=gamrow[:],
          scalar1=gamsc[0:1, 0:1], scalar2=None, op0=ALU.mult, deps=[t_in0])
        E("vector", "tensor_tensor_scan", out=grow[:], data0=gamrow[:],
          data1=zerrow[:], initial=1.0, op0=ALU.mult, op1=ALU.add)
        E("vector", "reciprocal", out=r1[0:1, 0:1], in_=gamsc[0:1, 0:1])
        E("vector", "memset", gamrow[:], 1.0)
        E("vector", "tensor_scalar", out=gamrow[:], in0=gamrow[:],
          scalar1=r1[0:1, 0:1], scalar2=None, op0=ALU.mult)
        tv_rows = E("vector", "tensor_tensor_scan", out=girow[:], data0=gamrow[:],
                    data1=zerrow[:], initial=1.0, op0=ALU.mult, op1=ALU.add)

        tvgci = E("vector", "tensor_scalar", out=gamrow[:], in0=girow[:],
                  scalar1=grow[0:1, 127:128], scalar2=None, op0=ALU.mult)
        E("tensor", "transpose", pD[:, 0:1], grow[:], iden[0:1, 0:1],
          deps=[tv_rows, tvgci])
        E("tensor", "transpose", pD[:, 128:129], gamrow[:], iden[0:1, 0:1])
        tt_g = E("tensor", "matmul", pD[:, 256:257], onesrow[:],
                 grow[0:1, 127:128], start=True, stop=True, inc=1)
        E("vector", "tensor_copy", out=gvec[:], in_=pD[:, 0:1], deps=[tt_g])
        E("vector", "tensor_copy", out=givec[:], in_=pD[:, 128:129])
        tv_g = E("vector", "tensor_copy", out=gcvec[:], in_=pD[:, 256:257])
        E("vector", "reciprocal", out=r1[:, 15:16], in_=gcvec[:])
        E("vector", "tensor_scalar", out=maskUT[:], in0=maskUT[:],
          scalar1=r1[:, 15:16], scalar2=None, op0=ALU.mult)

        for hh in range(2):
            sl = slice(hh * 512, (hh + 1) * 512)
            E("tensor", "matmul", pA[:, sl], grow[:], bqrow[:, sl], start=True,
              stop=True, deps=[tv_g, t_rows])
            E("tensor", "matmul", pB[:, sl], gamrow[:], bkrow[:, sl], start=True, stop=True)
            tt_b = E("tensor", "matmul", pC[:, sl], onesrow[:], bvrow[:, sl],
                     start=True, stop=True, inc=1)
        E("vector", "tensor_copy", out=gbq[:], in_=pA[:], deps=[tt_b])
        E("vector", "tensor_copy", out=gbk[:], in_=pB[:])
        tv_bias = E("vector", "tensor_copy", out=bvb[:], in_=pC[:])
        barrI = E.barrier_tokens()
        esI.close()

        sbA = lambda name, shape, dt=F32: esA.enter_context(nc.sbuf_tensor(name, shape, dt))
        WQTH = sbA("WQTH", [128, KT * D], F16)
        WQTL = sbA("WQTL", [128, KT * D], F16)
        WKTH = sbA("WKTH", [128, KT * D], F16)
        WKTL = sbA("WKTL", [128, KT * D], F16)
        WVTH = sbA("WVTH", [128, KT * D], F16)
        WVTL = sbA("WVTL", [128, KT * D], F16)
        S_sb = sbA("S_sb", [128, KT * D])
        z_sb = sbA("z_sb", [128, KT])
        xin = [sbA("xin0", [128, D]), sbA("xin1", [128, D])]
        bufA = sbA("bufA", [128, D])
        bufB = sbA("bufB", [128, D])
        dmp2 = sbA("dmp2", [128, D])
        xhatTh = sbA("xhatTh", [128, D], F16)
        xhatTl = sbA("xhatTl", [128, D], F16)
        qt = sbA("qt", [128, D])
        kt_ = sbA("kt", [128, D])
        kth16 = sbA("kth16", [128, D], F16)
        ktl16 = sbA("ktl16", [128, D], F16)
        vt = sbA("vt", [128, D])
        vh16 = sbA("vh16", [128, D], F16)
        vl16 = sbA("vl16", [128, D], F16)
        qf = sbA("qf", [128, D])
        kf = sbA("kf", [128, D])
        pmask = sbA("pmask", [128, 128])
        pmh = sbA("pmh", [128, 128], F16)
        pml = sbA("pml", [128, 128], F16)
        ones16 = sbA("ones16", [128, 1], F16)
        wstg = sbA("wstg", [128, 1024])
        wg1 = sbA("wg1", [128, 512], FP8)
        wg2 = sbA("wg2", [128, 512], FP8)
        absacc = sbA("absacc", [128, 64])
        wqr = sbA("wqr", [128, 4])
        tqA = esA.enter_context(nc.sbuf_tensor("tqA", [128, D], I32))
        xqA = sbA("xqA", [128, D], BF16)

        E("vector", "memset", S_sb[:], 0.0, deps=barrI)
        tv_z0 = E("vector", "memset", z_sb[:], 0.0)
        t_x0 = E("gdma", "dma_start", out=xin[0][:], in_=x_d[0:128, :], deps=barrI)
        E("vector", "tensor_copy", out=ones16[:], in_=onescol[:], deps=barrI)
        wsplit = {"rd": [None, None]}
        def load_w_pair(dram, WH, WL):
            last = None
            for kt in range(KT):
                h = kt % 2
                buf = wstg if h == 0 else qt
                qn = ("sync", "gdma", "sdma")[kt % 3]
                td = E(qn, "dma_start", out=buf[:],
                       in_=dram[kt * 128:(kt + 1) * 128, :],
                       deps=[wsplit["rd"][h]] + barrI)
                sl = slice(kt * 1024, (kt + 1) * 1024)
                E("vector", "tensor_copy", out=WH[:, sl], in_=buf[:], deps=[td])
                last = E("vector", "tensor_tensor", out=WL[:, sl], in0=buf[:],
                         in1=WH[:, sl], op=ALU.subtract)
                wsplit["rd"][h] = last
            return last
        t_wk = load_w_pair(wkt_d, WKTH, WKTL)
        t_wv = load_w_pair(wvt_d, WVTH, WVTL)
        t_wq = load_w_pair(wqt_d, WQTH, WQTL)

        tok = {
            "xq_ev": None, "xq_odd": None,
            "v15": tv_bias, "v16": None, "v17": None, "v21": tv_z0,
            "num": None, "den": None, "P": None,
            "q": None, "k": None, "v": t_wv,
            "trq": None, "trk": None, "trx": None,
            "L": [None] * KT, "z": None,
            "v19": None, "s5": None, "x2dma": None,
            "v9": None, "v10": None, "v11": None, "v12": None, "v13": None,
            "v14": tv_bias, "s3": None, "s4": None, "pDfree": tv_bias,
            "v20": [tv_z0] * KT, "v20last": tv_z0,
            "xin_rd": [[t_x0], []],
            "v8": None, "s1": None, "tskh": None, "xqdma": None,
        }

        def ln1(c, dep_dma):
            g = c % 2
            E("scalar", "activation", out=dmp2[:], in_=xin[g][:],
              func=AF.Copy, bias=0.0, scale=1.0,
              accum_out=r1[:, 0:1], deps=[dep_dma, tok["s5"]])
            tok["s1"] = E("scalar", "activation", out=bufB[:], in_=xin[g][:],
                          func=AF.Square, bias=0.0, scale=1.0,
                          accum_out=r1[:, 1:2], deps=[dep_dma, tok["v11"],
                                                      tok["trk"]])
            E("vector", "tensor_scalar", out=r1[:, 2:3], in0=r1[:, 0:1],
              scalar1=1.0 / D, scalar2=None, op0=ALU.mult, deps=[tok["s1"]])
            E("vector", "tensor_scalar", out=r1[:, 3:4], in0=r1[:, 1:2],
              scalar1=1.0 / D, scalar2=None, op0=ALU.mult, deps=[tok["s1"]])
            E("vector", "tensor_scalar", out=r1[:, 4:5], in0=r1[:, 2:3],
              scalar1=r1[:, 2:3], scalar2=None, op0=ALU.mult)
            tv = E("vector", "tensor_tensor", out=r1[:, 5:6], in0=r1[:, 3:4],
                   in1=r1[:, 4:5], op=ALU.subtract)
            ts = E("scalar", "activation", out=r1[:, 6:7], in_=r1[:, 5:6],
                   func=AF.Sqrt, bias=epsv[:, 0:1], scale=1.0, deps=[tv])
            E("vector", "reciprocal", out=r1[:, 7:8], in_=r1[:, 6:7], deps=[ts])
            E("vector", "tensor_scalar", out=r1[:, 8:9], in0=r1[:, 2:3],
              scalar1=r1[:, 7:8], scalar2=-1.0, op0=ALU.mult, op1=ALU.mult)
            tok["v8"] = E("vector", "tensor_scalar", out=bufA[:], in0=xin[g][:],
                          scalar1=r1[:, 7:8], scalar2=r1[:, 8:9], op0=ALU.mult,
                          op1=ALU.add, deps=[tok["trq"], tok["trx"]])
            tok["xin_rd"][g] = [tok["v8"]]

        def front(c):
            for dt_ in range(KT):
                tok_t = E("tensor", "transpose", pD[:, dt_ * 128:(dt_ + 1) * 128],
                          bufA[:, dt_ * 128:(dt_ + 1) * 128], iden[:],
                          deps=[tok["v8"], tok["pDfree"]] if dt_ == 0 else (),
                          inc=1 if dt_ == KT - 1 else 0)
            tok["trx"] = tok_t
            tsh = E("scalar", "copy", out=xhatTh[:], in_=pD[:],
                    deps=[tok["trx"], tok["v"]])
            tok["v9"] = E("vector", "tensor_tensor", out=xhatTl[:], in0=pD[:],
                          in1=xhatTh[:], op=ALU.subtract, deps=[tsh])
            tok["pDfree"] = tok["v9"]

        def ln2_stats(c):
            m = c - n_warm
            g = c % 2
            E("scalar", "activation", out=dmp2[:], in_=xin[g][:],
              func=AF.Copy, bias=0.0, scale=1.0,
              accum_out=r2[:, 2:3], deps=[tok["v19"]])
            tok["s5"] = E("scalar", "activation", out=dmp2[:], in_=xin[g][:],
                          func=AF.Square, bias=0.0, scale=1.0,
                          accum_out=r2[:, 3:4], deps=[tok["v19"]])
            E("vector", "tensor_reduce", out=r2[:, 4:5], in_=xin[g][:],
              axis=AX.X, op=ALU.max)
            E("vector", "tensor_reduce", out=r2[:, 5:6], in_=xin[g][:],
              axis=AX.X, op=ALU.min, negate=True)
            E("vector", "tensor_scalar", out=r2[:, 6:7], in0=r2[:, 2:3],
              scalar1=1.0 / D, scalar2=None, op0=ALU.mult, deps=[tok["s5"]])
            E("vector", "tensor_scalar", out=r2[:, 7:8], in0=r2[:, 3:4],
              scalar1=1.0 / D, scalar2=None, op0=ALU.mult, deps=[tok["s5"]])
            E("vector", "tensor_scalar", out=r2[:, 8:9], in0=r2[:, 6:7],
              scalar1=r2[:, 6:7], scalar2=None, op0=ALU.mult)
            tvv = E("vector", "tensor_tensor", out=r2[:, 9:10], in0=r2[:, 7:8],
                    in1=r2[:, 8:9], op=ALU.subtract)
            tss = E("scalar", "activation", out=r2[:, 10:11], in_=r2[:, 9:10],
                    func=AF.Sqrt, bias=epsv[:, 0:1], scale=1.0, deps=[tvv])
            E("vector", "reciprocal", out=r2[:, 11:12], in_=r2[:, 10:11],
              deps=[tss])
            E("vector", "tensor_tensor", out=r2[:, 12:13], in0=r2[:, 4:5],
              in1=r2[:, 6:7], op=ALU.subtract)
            E("vector", "tensor_tensor", out=r2[:, 13:14], in0=r2[:, 5:6],
              in1=r2[:, 6:7], op=ALU.add)
            E("vector", "tensor_tensor", out=r2[:, 12:13], in0=r2[:, 12:13],
              in1=r2[:, 13:14], op=ALU.max)
            E("vector", "tensor_scalar", out=r2[:, 12:13], in0=r2[:, 12:13],
              scalar1=r2[:, 11:12], scalar2=EPS_Q, op0=ALU.mult, op1=ALU.max)
            E("vector", "reciprocal", out=r2[:, 14:15], in_=r2[:, 12:13])
            E("vector", "tensor_scalar", out=r2[:, 14:15], in0=r2[:, 14:15],
              scalar1=QB, scalar2=None, op0=ALU.mult)
            E("vector", "tensor_scalar", out=stats[:, 3 * m:3 * m + 1],
              in0=r2[:, 11:12], scalar1=r2[:, 14:15], scalar2=None,
              op0=ALU.mult)
            E("vector", "tensor_scalar", out=stats[:, 3 * m + 1:3 * m + 2],
              in0=r2[:, 6:7], scalar1=stats[:, 3 * m:3 * m + 1], scalar2=-1.0,
              op0=ALU.mult, op1=ALU.mult)
            tvs = E("vector", "tensor_scalar",
                    out=stats[:, 3 * m + 2:3 * m + 3],
                    in0=r2[:, 12:13], scalar1=1.0 / QB, scalar2=None,
                    op0=ALU.mult)
            sca = stats[:, 3 * m:3 * m + 1]
            scb = stats[:, 3 * m + 1:3 * m + 2]
            sv = E("scalar", "activation", out=vt[:], in_=xin[g][:],
                   func=AF.Identity, scale=sca, bias=scb, deps=[tvs])
            st = E("scalar", "activation", out=tqA[:], in_=xin[g][:],
                   func=AF.Identity, scale=sca, bias=scb)
            E("vector", "scalar_tensor_tensor", out=qt[:], in0=tqA[:],
              scalar=0.5, in1=vt[:], op0=ALU.add, op1=ALU.is_le,
              deps=[sv, st, wq_state["rd"][1]])
            E("vector", "scalar_tensor_tensor", out=kt_[:], in0=tqA[:],
              scalar=-0.5, in1=vt[:], op0=ALU.add, op1=ALU.is_ge,
              deps=[tok["tskh"]])
            tst2 = E("vector", "scalar_tensor_tensor", out=vt[:], in0=qt[:],
                     scalar=1.0, in1=tqA[:], op0=ALU.mult, op1=ALU.add)
            wq_state["rd"][1] = tst2
            txq = E("vector", "tensor_tensor", out=xqA[:], in0=vt[:],
                    in1=kt_[:], op=ALU.subtract, deps=[tok["xqdma"]])
            tok["xqdma"] = E("gdma", "dma_start",
                             out=xqs_d[m * 128:(m + 1) * 128, :],
                             in_=xqA[:], deps=[txq])
            tok["xin_rd"][g] = [tvs, st, tok["x2dma"]]

        def wq_sub(pt):
            if wq_state.get("ws_pending") and pt == 0:
                wq_state["ws_pending"] = False
                wq_ws_compute()
            kindw, parts = wq_state["mid"]
            if kindw is None or pt >= len(parts):
                return
            items = parts[pt]
            if not items:
                return
            if kindw == "p1":
                wq_pass1_slice(items)
            else:
                wq_pass2_slice(items)

        def chunk(c):
            warm = c < n_warm
            m = c - n_warm
            g = c % 2
            if c + 1 < n_tot:
                gn = (c + 1) % 2
                tok_dma_next = E("gdma", "dma_start", out=xin[gn][:],
                                 in_=x_d[(c + 1) * 128:(c + 2) * 128, :],
                                 deps=tok["xin_rd"][gn])
            else:
                tok_dma_next = None

            def proj(WH, WL, wtok, pdst, dep_free):
                t = None
                for hh in range(2):
                    for kk in range(KT):
                        co = kk * D + hh * 512
                        for ph, (lh, rh) in enumerate(
                                ((xhatTh, WH), (xhatTl, WH), (xhatTh, WL))):
                            t = E("tensor", "matmul",
                                  pdst[:, hh * 512:(hh + 1) * 512],
                                  lh[:, kk * 128:(kk + 1) * 128],
                                  rh[:, co:co + 512],
                                  start=(kk == 0 and ph == 0),
                                  stop=(kk == KT - 1 and ph == 2),
                                  deps=([tok["v9"], wtok] + dep_free)
                                  if (hh == 0 and kk == 0 and ph == 0) else (),
                                  inc=1 if (hh == 1 and kk == KT - 1 and ph == 2) else 0)
                return t

            if not warm:
                tok["q"] = proj(WQTH, WQTL, t_wq, pA, [tok["xq_ev"], tok["v19"]])
                tok["v10"] = E("vector", "scalar_tensor_tensor", out=bufA[:],
                               in0=pA[:], scalar=gvec[:, 0:1], in1=gbq[:],
                               op0=ALU.mult, op1=ALU.add,
                               deps=[tok["q"], tok["trx"]])
            tok["k"] = proj(WKTH, WKTL, t_wk, pB, [tok["v16"], tok["v21"]])
            tok["v11"] = E("vector", "scalar_tensor_tensor", out=bufB[:],
                           in0=pB[:], scalar=givec[:, 0:1], in1=gbk[:],
                           op0=ALU.mult, op1=ALU.add,
                           deps=[tok["k"], tok["s1"]])
            tok["s4"] = E("scalar", "activation", out=kt_[:], in_=bufB[:],
                          func=AF.Relu, bias=0.0, scale=1.0,
                          deps=[tok["v11"]])
            tok["v"] = proj(WVTH, WVTL, t_wv, pC, [tok["xq_odd"], tok["v12"]])
            tvt = E("vector", "tensor_tensor", out=vt[:], in0=pC[:],
                    in1=bvb[:], op=ALU.add, deps=[tok["v"], tok["L"][KT - 1]])
            if not warm:
                for dt_ in range(KT):
                    tok_t = E("tensor", "transpose", pD[:, dt_ * 128:(dt_ + 1) * 128],
                              bufA[:, dt_ * 128:(dt_ + 1) * 128], iden[:],
                              deps=[tok["v10"], tok["pDfree"], tok["den"]]
                              if dt_ == 0 else (),
                              inc=1 if dt_ == KT - 1 else 0)
                tok["trq"] = tok_t
                tok["v14"] = E("vector", "tensor_scalar", out=qf[:], in0=pD[:],
                               scalar1=0.0, scalar2=None, op0=ALU.max,
                               deps=[tok["trq"]])
                tok["pDfree"] = tok["v14"]
                for dt_ in range(KT):
                    tok_t = E("tensor", "transpose", pB[:, dt_ * 128:(dt_ + 1) * 128],
                              bufB[:, dt_ * 128:(dt_ + 1) * 128], iden[:],
                              deps=[tok["v11"]] if dt_ == 0 else (),
                              inc=1 if dt_ == KT - 1 else 0)
                tok["trk"] = tok_t
                tok["v15"] = E("scalar", "activation", out=kf[:], in_=pB[:, 0:1024],
                               func=AF.Relu, bias=0.0, scale=1.0,
                               deps=[tok["trk"], tok["P"]])
            tskh = E("scalar", "copy", out=kth16[:], in_=kt_[:], deps=[tok["z"]])
            tok["tskh"] = tskh
            tsvh = E("scalar", "copy", out=vh16[:], in_=vt[:],
                     deps=[tok["L"][KT - 1]])
            if not warm:
                for kk in range(KT):
                    tok_t = E("tensor", "matmul", pB[:, 0:128],
                              kf[:, kk * 128:(kk + 1) * 128],
                              qf[:, kk * 128:(kk + 1) * 128],
                              start=(kk == 0), stop=(kk == KT - 1),
                              deps=[tok["v15"], tok["v14"], tok["v11"]] if kk == 0 else (),
                              inc=1 if kk == KT - 1 else 0)
                tok["P"] = tok_t
                E("vector", "tensor_tensor", out=pmask[:],
                  in0=pB[:, 0:128], in1=maskUT[:], op=ALU.mult,
                  deps=[tok["P"]])
                tsph = E("scalar", "copy", out=pmh[:], in_=pmask[:])
                tok["v16"] = E("vector", "tensor_tensor", out=pml[:],
                               in0=pmask[:], in1=pmh[:], op=ALU.subtract,
                               deps=[tsph])
            tok["v13"] = E("vector", "tensor_tensor", out=ktl16[:], in0=kt_[:],
                           in1=kth16[:], op=ALU.subtract,
                           deps=[tskh, tok["z"]])
            tok["v12"] = E("vector", "tensor_tensor", out=vl16[:], in0=vt[:],
                           in1=vh16[:], op=ALU.subtract, deps=[tsvh])
            if c + 1 < n_tot:
                ln1(c + 1, tok_dma_next)
            wq_sub(0)

            if not warm:
                for hh in range(2):
                    sl = slice(hh * 512, (hh + 1) * 512)
                    for it in range(KT):
                        E("tensor", "matmul", pA[:, sl],
                          qf[:, it * 128:(it + 1) * 128],
                          S_sb[:, it * D + hh * 512: it * D + hh * 512 + 512],
                          start=(it == 0), stop=False,
                          deps=[tok["v14"], tok["v20last"], tok["v10"]]
                          if (hh == 0 and it == 0) else (), inc=0)
                    for ph, (lh, rh) in enumerate(
                            ((pmh, vh16), (pml, vh16), (pmh, vl16))):
                        tok_t = E("tensor", "matmul", pA[:, sl], lh[:], rh[:, sl],
                                  start=False, stop=(ph == 2),
                                  deps=[tok["v16"], tok["v12"]]
                                  if (hh == 0 and ph == 0) else (),
                                  inc=1 if (hh == 1 and ph == 2) else 0)
                tok["num"] = tok_t
                for it in range(KT):
                    E("tensor", "matmul", pB[:, 512:513],
                      qf[:, it * 128:(it + 1) * 128], z_sb[:, it:it + 1],
                      start=(it == 0), stop=False,
                      deps=[tok["v21"], tok["v17"], tok["v15"]] if it == 0 else (),
                      inc=0)
                E("tensor", "matmul", pB[:, 512:513], pmh[:], ones16[:],
                  start=False, stop=False)
                tok["den"] = E("tensor", "matmul", pB[:, 512:513], pml[:],
                               ones16[:], start=False, stop=True, inc=1)
                tok["v17"] = E("vector", "tensor_scalar", out=r2[:, 0:1],
                               in0=pB[:, 512:513], scalar1=EPS_DEN, scalar2=None,
                               op0=ALU.add, deps=[tok["den"]])
                E("vector", "reciprocal", out=r2[:, 1:2], in_=r2[:, 0:1])
                tok["v19"] = E("vector", "scalar_tensor_tensor", out=xin[g][:],
                               in0=pA[:], scalar=r2[:, 1:2], in1=xin[g][:],
                               op0=ALU.mult, op1=ALU.add,
                               deps=[tok["num"], tok["s5"], tok["x2dma"]])
                tok["x2dma"] = E("gdma", "dma_start",
                                 out=x2s_d[m * 128:(m + 1) * 128, :],
                                 in_=xin[g][:], deps=[tok["v19"]])

            if c + 1 < n_tot:
                front(c + 1)
            wq_sub(1)

            for it in range(KT):
                pdst = pA if it % 2 == 0 else pC
                if it == 0:
                    dfree = [tok["v13"], tok["v12"], tok["xq_ev"],
                             (tok["v19"] if not warm else None)]
                elif it == 1:
                    dfree = [tok["xq_odd"]]
                else:
                    dfree = [tok["v20"][it - 2]]
                for hh in range(2):
                    sl = slice(hh * 512, (hh + 1) * 512)
                    ksl = slice(it * 128, (it + 1) * 128)
                    for ph, (lh, rh) in enumerate(
                            ((kth16, vh16), (ktl16, vh16), (kth16, vl16))):
                        tok_t = E("tensor", "matmul", pdst[:, sl],
                                  lh[:, ksl], rh[:, sl],
                                  start=(ph == 0), stop=(ph == 2),
                                  deps=dfree if (hh == 0 and ph == 0) else (),
                                  inc=1 if (hh == 1 and ph == 2) else 0)
                tok["L"][it] = tok_t
                tok["v20"][it] = E("vector", "scalar_tensor_tensor",
                                   out=S_sb[:, it * D:(it + 1) * D],
                                   in0=S_sb[:, it * D:(it + 1) * D],
                                   scalar=gcvec[:, 0:1], in1=pdst[:],
                                   op0=ALU.mult, op1=ALU.add,
                                   deps=[tok["L"][it], tok["num"]])
            tok["v20last"] = tok["v20"][KT - 1]
            tok["xq_ev"] = tok["v20"][KT - 2]
            tok["xq_odd"] = tok["v20"][KT - 1]

            wq_sub(2)

            for it in range(KT):
                ksl = slice(it * 128, (it + 1) * 128)
                for ph, lh in enumerate((kth16, ktl16)):
                    tok_t = E("tensor", "matmul", pB[:, 520 + it:521 + it],
                              lh[:, ksl], ones16[:],
                              start=(it == 0 and ph == 0),
                              stop=(it == KT - 1 and ph == 1),
                              deps=[tok["v17"], tok["v13"], tok["v11"]]
                              if (it == 0 and ph == 0) else (),
                              inc=1 if (it == KT - 1 and ph == 1) else 0)
            tok["z"] = tok_t
            tok["v21"] = E("vector", "scalar_tensor_tensor", out=z_sb[:],
                           in0=z_sb[:], scalar=gcvec[:, 0:1], in1=pB[:, 520:528],
                           op0=ALU.mult, op1=ALU.add,
                           deps=[tok["z"], tok["den"]])
            wq_sub(3)

            if warm and c == n_warm - 1:
                E("vector", "tensor_scalar", out=S_sb[:], in0=S_sb[:],
                  scalar1=maskvec[:, 0:1], scalar2=None, op0=ALU.mult,
                  deps=[tok["v20last"], t_in1])
                tok["v21"] = E("vector", "tensor_scalar", out=z_sb[:], in0=z_sb[:],
                               scalar1=maskvec[:, 0:1], scalar2=None, op0=ALU.mult)
                tok["v20last"] = tok["v21"]
                tok["xq_ev"] = tok["v21"]
                tok["xq_odd"] = tok["v21"]



        def w_subtiles(which):
            if which == 1:
                nj = H // 1024
                for i in range(KT):
                    for j in range(nj):
                        yield (w1t_d[i * 128:(i + 1) * 128, j * 1024:(j + 1) * 1024],
                               i * nj + j)
            else:
                for i in range(H // 128):
                    yield w2t_d[i * 128:(i + 1) * 128, :], i

        n1 = KT * (H // 1024)
        n2 = H // 128
        subs = ([(ap, 0, idx) for ap, idx in w_subtiles(1)]
                + [(ap, n1, idx) for ap, idx in w_subtiles(2)])
        nsub = len(subs)
        wq_state = {"rd": [wsplit["rd"][0], wsplit["rd"][1]], "n": 0, "w8dma": None,
                    "wg_n": 0, "w8": [None, None],
                    "wsb": None, "abslast": None, "mid": (None, ())}

        def wq_pass1_slice(items):
            for ap, base, idx in items:
                b = wq_state["n"] % 2
                wq_state["n"] += 1
                buf = wstg if b == 0 else qt
                td = E("sync", "dma_start", out=buf[:],
                       in_=ap, deps=[wq_state["rd"][b]])
                t = E("scalar", "activation", out=dmp2[:], in_=buf[:],
                      func=AF.Abs, bias=0.0, scale=1.0,
                      accum_out=absacc[:, base + idx:base + idx + 1], deps=[td])
                wq_state["rd"][b] = t
                wq_state["abslast"] = t

        def wq_ws_compute():
            E("vector", "tensor_reduce", out=wqr[:, 0:1], in_=absacc[:, 0:n1],
              axis=AX.X, op=ALU.add, deps=[wq_state["abslast"]])
            tred = E("vector", "tensor_reduce", out=wqr[:, 1:2],
                     in_=absacc[:, n1:n1 + n2], axis=AX.X, op=ALU.add)
            tsum = E("tensor", "matmul", pD[0:1, 256:258], onescol[:],
                     wqr[:, 0:2], start=True, stop=True,
                     deps=[tok["pDfree"], tred], inc=1)
            tgc = E("vector", "tensor_copy", out=wqr[0:1, 2:4],
                    in_=pD[0:1, 256:258], deps=[tsum])
            E("vector", "tensor_scalar", out=wqr[0:1, 2:3], in0=wqr[0:1, 2:3],
              scalar1=1.0 / (D * H), scalar2=EPS_Q, op0=ALU.mult, op1=ALU.max,
              deps=[tgc])
            tv = E("vector", "tensor_scalar", out=wqr[0:1, 3:4], in0=wqr[0:1, 3:4],
                   scalar1=1.0 / (D * H), scalar2=EPS_Q, op0=ALU.mult, op1=ALU.max)
            E("tensor", "matmul", pD[:, 0:1], onesrow[:], wqr[0:1, 2:3],
              start=True, stop=False, deps=[tok["pDfree"], tv])
            tbc = E("tensor", "matmul", pD[:, 128:129], onesrow[:], wqr[0:1, 3:4],
                    start=False, stop=True, inc=1)
            E("vector", "tensor_copy", out=ws1b[:], in_=pD[:, 0:1], deps=[tbc])
            E("vector", "tensor_copy", out=ws2b[:], in_=pD[:, 128:129])
            E("vector", "tensor_scalar", out=hws1b[:, 0:1], in0=ws1b[:],
              scalar1=0.5, scalar2=None, op0=ALU.mult)
            E("vector", "tensor_scalar", out=hws1b[:, 1:2], in0=hws1b[:, 0:1],
              scalar1=-1.0, scalar2=None, op0=ALU.mult)
            E("vector", "tensor_scalar", out=hws2b[:, 0:1], in0=ws2b[:],
              scalar1=0.5, scalar2=None, op0=ALU.mult)
            E("vector", "tensor_scalar", out=hws2b[:, 1:2], in0=hws2b[:, 0:1],
              scalar1=-1.0, scalar2=None, op0=ALU.mult)
            tv2 = E("vector", "tensor_scalar", out=w2s127[:], in0=ws2b[:],
                    scalar1=1.0 / QB, scalar2=None, op0=ALU.mult)
            wq_state["wsb"] = tv2
            tok["pDfree"] = tv2

        def wq_pass2_slice(items):
            for ap, base, idx in items:
                which = 1 if base == 0 else 2
                hb = hws1b if which == 1 else hws2b
                b = wq_state["n"] % 2
                wq_state["n"] += 1
                buf = wstg if b == 0 else qt
                td = E("sync", "dma_start", out=buf[:],
                       in_=ap, deps=[wq_state["rd"][b]])
                for hf in range(2):
                    bsl = slice(hf * 512, (hf + 1) * 512)
                    p8 = wq_state["wg_n"] % 2
                    wq_state["wg_n"] += 1
                    bge = wg1 if p8 == 0 else wg2
                    ble = wg2 if p8 == 0 else wg1
                    tg1 = E("vector", "tensor_scalar", out=bge[:], in0=buf[:, bsl],
                            scalar1=hb[:, 0:1], scalar2=None, op0=ALU.is_ge,
                            deps=[td, wq_state["wsb"], wq_state["w8"][p8]])
                    tg2 = E("vector", "tensor_scalar", out=ble[:], in0=buf[:, bsl],
                            scalar1=hb[:, 1:2], scalar2=None, op0=ALU.is_le,
                            deps=[wq_state["w8"][1 - p8]])
                    wq_state["rd"][b] = tg2
                    tv = E("vector", "tensor_tensor", out=bge[:], in0=bge[:],
                           in1=ble[:], op=ALU.subtract, deps=[tg1, tg2])
                    if which == 1:
                        nj = H // 1024
                        i, j = idx // nj, idx % nj
                        dst = w1q_d[i * 128:(i + 1) * 128,
                                    j * 1024 + hf * 512:j * 1024 + (hf + 1) * 512]
                    else:
                        dst = w2q_d[idx * 128:(idx + 1) * 128, bsl]
                    wq_state["w8"][p8] = E("gdma", "dma_start", out=dst,
                                           in_=bge[:], deps=[tv])
                    wq_state["w8dma"] = wq_state["w8"][p8]

        p1e = max(1, min(16, n_tot // 2))
        cnts = [0] * n_tot
        nw = min(n_warm, p1e)
        rem = nsub
        if p1e - nw >= 4:
            for c in range(nw):
                cnts[c] = min(rem, 2)
                rem -= cnts[c]
        m1 = list(range(nw, p1e))
        for i, c in enumerate(m1):
            t = -(-rem // (len(m1) - i))
            cnts[c] = t
            rem -= t
        if rem:
            cnts[p1e - 1] += rem
        rem = nsub
        m2 = list(range(p1e, n_tot))
        for i, c in enumerate(m2):
            t = -(-rem // (len(m2) - i))
            cnts[c] = t
            rem -= t

        ln1(0, t_x0)
        front(0)
        o1 = o2 = 0
        for c in range(n_tot):
            if c < p1e:
                items = subs[o1:o1 + cnts[c]]
                o1 += cnts[c]
                kindw = "p1"
            else:
                items = subs[o2:o2 + cnts[c]]
                o2 += cnts[c]
                kindw = "p2"
            q4 = -(-len(items) // 4) if items else 1
            wq_state["mid"] = (kindw if items else None,
                               [items[i:i + q4] for i in range(0, len(items), q4)])
            chunk(c)
            if c >= n_warm:
                ln2_stats(c)
            if c == p1e - 1:
                wq_state["ws_pending"] = True

        tok_wq = wq_state["w8dma"]
        barrA = E.barrier_tokens()
        esA.close()

        esB = ExitStack()
        sbB = lambda name, shape, dt=F32: esB.enter_context(nc.sbuf_tensor(name, shape, dt))
        W1Q = sbB("W1Q", [128, KT * H], FP8)
        W2Q = sbB("W2Q", [128, (H // 128) * D], FP8)
        b1b = sbB("b1b", [128, H])
        b2b = sbB("b2b", [128, D])
        x2b = [sbB("x2b0", [128, D]), sbB("x2b1", [128, D]), sbB("x2b2", [128, D])]
        vq = [sbB("vq0", [128, D]), sbB("vq1", [128, D])]
        tq = [esB.enter_context(nc.sbuf_tensor("tq0", [128, D], I32)),
              esB.enter_context(nc.sbuf_tensor("tq1", [128, D], I32))]
        cbuf = sbB("cbuf", [128, D])
        cbuf2 = sbB("cbuf2", [128, D])
        xq_tm = sbB("xq_tm", [128, D], BF16)
        xqT = sbB("xqT", [128, D], BF16)
        hdeq = [sbB("hdeq0", [128, H]), sbB("hdeq1", [128, H])]
        hq_tm = sbB("hq_tm", [128, H], BF16)
        hqT = sbB("hqT", [128, H], BF16)
        outb = [sbB("outb0", [128, D]), sbB("outb1", [128, D])]
        pH = [esB.enter_context(nc.psum_tensor("pH0", [128, 512], F32)),
              esB.enter_context(nc.psum_tensor("pH1", [128, 512], F32))]
        pO = esB.enter_context(nc.psum_tensor("pO", [128, 1024], F32))
        pT1 = esB.enter_context(nc.psum_tensor("pT1", [128, 2048], BF16))
        pT2 = esB.enter_context(nc.psum_tensor("pT2", [128, 2048], BF16))
        b1row = sbB("b1rowB", [1, H])
        b2row = sbB("b2rowB", [1, D])

        E("sync", "dma_start", out=b1row[:], in_=b1_d[:], deps=barrA)
        t_rowsB = E("sync", "dma_start", out=b2row[:], in_=b2_d[:])
        t_x2b0 = E("sync", "dma_start", out=x2b[0][:], in_=x2s_d[0:128, :])
        w1l, w2l = [], []
        for kt in range(KT):
            w1l.append(E("sync" if kt % 2 == 0 else "gdma", "dma_start",
                         out=W1Q[:, kt * H:(kt + 1) * H],
                         in_=w1q_d[kt * 128:(kt + 1) * 128, :], deps=barrA))
        for kt in range(H // 128):
            w2l.append(E("sync" if kt % 2 == 0 else "gdma", "dma_start",
                         out=W2Q[:, kt * D:(kt + 1) * D],
                         in_=w2q_d[kt * 128:(kt + 1) * 128, :], deps=barrA))

        tprev = None
        for hh in range(KT):
            sl = slice(hh * 512, (hh + 1) * 512)
            tt_b1 = E("tensor", "matmul", pH[hh % 2][:], onesrow[:], b1row[:, sl],
                      start=True, stop=True,
                      deps=barrA + [t_rowsB] + ([tprev] if tprev else []), inc=1)
            tprev = E("vector", "tensor_copy", out=b1b[:, sl], in_=pH[hh % 2][:],
                      deps=[tt_b1])
        for hh in range(2):
            sl = slice(hh * 512, (hh + 1) * 512)
            tt_b2 = E("tensor", "matmul", pO[:, sl], onesrow[:], b2row[:, sl],
                      start=True, stop=True, inc=1)
        tv_b2 = E("vector", "tensor_copy", out=b2b[:], in_=pO[:], deps=[tt_b2])

        E.strict_scalar = False
        tokB = {
            "trx": None, "trh_t": [None, None],
            "s_xqT": tprev, "s_hqT1": tprev, "s_hqT2": tprev,
            "dq": [tprev, tprev], "epi": [tv_b2, tv_b2],
            "x2rd": [None, None, None], "ydma": [None, None],
            "x2w": [t_x2b0, None, None],
            "h1last": None, "o": tv_b2, "tv_hq": None,
            "chain_last": [None, None],
            "chain_s1": [None, None], "chain_s2": [None, None],
            "rqh": None, "hql01": None, "lead_last": None, "chain_tsc": None,
        }

        def quant_chain(par, src_ap, sca, scb, n, out_bf, war_dep, deps0=(),
                        leads=None):
            if leads is not None:
                tv0, tsc = leads
            else:
                d0 = list(deps0)
                tv0 = E("scalar", "activation", out=vq[par][:, 0:n], in_=src_ap,
                        func=AF.Identity, scale=sca,
                        bias=(scb if scb is not None else 0.0),
                        deps=d0 + [tokB["chain_s1"][par]])
                tsc = E("scalar", "activation", out=tq[par][:, 0:n], in_=src_ap,
                        func=AF.Identity, scale=sca,
                        bias=(scb if scb is not None else 0.0),
                        deps=d0 + [tokB["chain_s2"][par]])
                tokB["chain_tsc"] = tsc
            E("vector", "scalar_tensor_tensor", out=cbuf[:, 0:n],
              in0=tq[par][:, 0:n], scalar=0.5, in1=vq[par][:, 0:n],
              op0=ALU.add, op1=ALU.is_le, deps=[tv0, tsc])
            tokB["chain_s1"][par] = E("vector", "scalar_tensor_tensor",
                                      out=cbuf2[:, 0:n],
                                      in0=tq[par][:, 0:n], scalar=-0.5,
                                      in1=vq[par][:, 0:n],
                                      op0=ALU.add, op1=ALU.is_ge)
            tokB["chain_s2"][par] = E("vector", "scalar_tensor_tensor",
                                      out=cbuf[:, 0:n],
                                      in0=cbuf[:, 0:n], scalar=1.0,
                                      in1=tq[par][:, 0:n],
                                      op0=ALU.mult, op1=ALU.add)
            t = E("vector", "tensor_tensor", out=out_bf, in0=cbuf[:, 0:n],
                  in1=cbuf2[:, 0:n], op=ALU.subtract, deps=[war_dep])
            tokB["chain_last"][par] = t
            return t

        def emit_prefetch(mq):
            if 0 < mq < n_main:
                g3 = mq % 3
                tokB["x2w"][g3] = E("sync", "dma_start", out=x2b[g3][:],
                                    in_=x2s_d[mq * 128:(mq + 1) * 128, :],
                                    deps=[tokB["x2rd"][g3]])


        def emit_hq_leads(mh):
            if 0 <= mh < n_main:
                p = mh % 2
                tokB["hql01"] = []
                for sub in range(2):
                    par = (sub + 1) % 2
                    sl = slice(sub * 1024, (sub + 1) * 1024)
                    tv0 = E("scalar", "activation", out=vq[par][:],
                            in_=hdeq[p][:, sl], func=AF.Identity,
                            scale=r2[:, 2:3], bias=0.0,
                            deps=[tokB["rqh"], tokB["chain_s1"][par]])
                    tsc = E("scalar", "activation", out=tq[par][:],
                            in_=hdeq[p][:, sl], func=AF.Identity,
                            scale=r2[:, 2:3], bias=0.0,
                            deps=[tokB["rqh"], tokB["chain_s2"][par]])
                    tokB["hql01"].append((tv0, tsc))

        def emit_chains(mf, mh):
            if 0 <= mh < n_main:
                p = mh % 2
                for sub in range(4):
                    sl = slice(sub * 1024, (sub + 1) * 1024)
                    tokB["tv_hq"] = quant_chain(
                        (sub + 1) % 2, hdeq[p][:, sl], r2[:, 2:3], None, 1024,
                        hq_tm[:, sl], tokB["trh_t"][0 if sub < 2 else 1],
                        deps0=[tokB["rqh"]],
                        leads=tokB["hql01"][sub] if sub < 2 else None)
                    if sub >= 2:
                        tokB["lead_last"] = tokB["chain_tsc"]

        def emit_front(mf):
            if not (0 <= mf < n_main):
                return
            pf = mf % 2
            rpartf = r2 if pf == 0 else r1
            sdq1c = r2[:, 0:1] if pf == 0 else r2[:, 14:15]
            for dt_ in range(KT):
                tok_t = E("tensor", "transpose",
                          pT2[:, dt_ * 128:(dt_ + 1) * 128],
                          xq_tm[:, dt_ * 128:(dt_ + 1) * 128], idenb[:],
                          deps=[tokB["xqw"], tokB["s_hqT2"]] if dt_ == 0 else (),
                          inc=1 if dt_ == KT - 1 else 0)
            tokB["trx"] = tok_t
            if mf + 1 < n_main:
                tokB["xqw"] = E("gdma", "dma_start", out=xq_tm[:],
                                in_=xqs_d[(mf + 1) * 128:(mf + 2) * 128, :],
                                deps=[tok_t])
            tokB["s_xqT"] = E("scalar", "copy", out=xqT[:], in_=pT2[:, 0:1024],
                              deps=[tok_t, tokB["h1last"]])
            tv_s1 = E("vector", "tensor_scalar", out=sdq1c,
                      in0=stats[:, 3 * mf + 2:3 * mf + 3],
                      scalar1=ws1b[:, 0:1], scalar2=None, op0=ALU.mult)
            tgel = {}
            def part_max(hh):
                E("vector", "tensor_reduce", out=rpartf[:, 5 + hh:6 + hh],
                  in_=hdeq[pf][:, hh * 512:(hh + 1) * 512], axis=AX.X,
                  op=ALU.max, apply_absolute_value=True, deps=[tgel[hh]])
            tg = None
            for hh in range(KT):
                pg = hh % 2
                for dt_ in range(KT):
                    d_ = ([tokB["s_xqT"], tokB["dq"][pg]] if dt_ == 0 else [])
                    if mf == 0 and hh == 0:
                        d_ = d_ + [w1l[dt_]]
                    tok_t = E("tensor", "matmul", pH[pg][:],
                              xqT[:, dt_ * 128:(dt_ + 1) * 128],
                              W1Q[:, dt_ * H + hh * 512: dt_ * H + hh * 512 + 512],
                              start=(dt_ == 0), stop=(dt_ == KT - 1),
                              deps=d_,
                              inc=1 if dt_ == KT - 1 else 0)
                sl = slice(hh * 512, (hh + 1) * 512)
                tokB["dq"][pg] = E("vector", "scalar_tensor_tensor",
                                   out=hdeq[pf][:, sl], in0=pH[pg][:],
                                   scalar=sdq1c, in1=b1b[:, sl],
                                   op0=ALU.mult, op1=ALU.add,
                                   deps=[tok_t, tv_s1])
                tg = E("scalar", "activation", out=hdeq[pf][:, sl],
                       in_=hdeq[pf][:, sl], func=AF.Gelu, bias=0.0,
                       scale=1.0, deps=[tokB["dq"][pg], tg])
                tgel[hh] = tg
                if hh >= 1:
                    part_max(hh - 1)
            part_max(KT - 1)
            tokB["h1last"] = tok_t
            E("vector", "tensor_reduce", out=r2[:, 1:2], in_=rpartf[:, 5:13],
              axis=AX.X, op=ALU.max)
            E("vector", "tensor_scalar", out=r2[:, 1:2], in0=r2[:, 1:2],
              scalar1=EPS_Q, scalar2=None, op0=ALU.max)
            E("vector", "reciprocal", out=r2[:, 2:3], in_=r2[:, 1:2],
              deps=[tokB["lead_last"]])
            tokB["rqh"] = E("vector", "tensor_scalar", out=r2[:, 2:3],
                            in0=r2[:, 2:3], scalar1=QB, scalar2=None,
                            op0=ALU.mult)
            E("vector", "tensor_scalar", out=r2[:, 3 + pf:4 + pf],
              in0=r2[:, 1:2], scalar1=w2s127[:, 0:1], scalar2=None,
              op0=ALU.mult)

        tokB["xqw"] = E("gdma", "dma_start", out=xq_tm[:], in_=xqs_d[0:128, :],
                        deps=barrA)
        emit_chains(0, -1)
        emit_front(0)
        emit_prefetch(1)

        for m in range(n_main + 1):
            emit_hq_leads(m)
            if m >= 1:
                for grp in range(2):
                    pdst = pT1 if grp == 0 else pT2
                    dep0 = ([tokB["tv_hq"], tokB["s_hqT1"]] if grp == 0
                            else [tokB["tv_hq"], tokB["s_xqT"]])
                    for j in range(16):
                        dt_ = grp * 16 + j
                        tok_t = E("tensor", "transpose",
                                  pdst[:, j * 128:(j + 1) * 128],
                                  hq_tm[:, dt_ * 128:(dt_ + 1) * 128], idenb[:],
                                  deps=dep0 if j == 0 else (),
                                  inc=1 if j == 15 else 0)
                    tokB["trh_t"][grp] = tok_t
                    if grp == 0:
                        tokB["s_hqT1"] = E("scalar", "copy", out=hqT[:, 0:2048],
                                           in_=pT1[:], deps=[tok_t, tokB["o"]])
                    else:
                        tokB["s_hqT2"] = E("scalar", "copy", out=hqT[:, 2048:4096],
                                           in_=pT2[:], deps=[tok_t, tokB["o"]])

            if m >= 1:
                mp = m - 1
                gp = mp % 3
                gpy = mp % 2
                for hh in range(2):
                    sl = slice(hh * 512, (hh + 1) * 512)
                    for kk in range(H // 128):
                        d_ = ([tokB["s_hqT1"], tokB["s_hqT2"],
                               tokB["epi"][hh]] if kk == 0 else [])
                        if mp == 0 and hh == 0:
                            d_ = d_ + [w2l[kk]]
                        tok_t = E("tensor", "matmul", pO[:, sl],
                                  hqT[:, kk * 128:(kk + 1) * 128],
                                  W2Q[:, kk * D + hh * 512: kk * D + hh * 512 + 512],
                                  start=(kk == 0), stop=(kk == H // 128 - 1),
                                  deps=d_,
                                  inc=1 if kk == H // 128 - 1 else 0)
                tokB["o"] = tok_t
            emit_chains(m + 1, m)
            if m >= 1:
                for hh in range(2):
                    sl = slice(hh * 512, (hh + 1) * 512)
                    E("vector", "scalar_tensor_tensor", out=outb[gpy][:, sl],
                      in0=pO[:, sl], scalar=r2[:, 3 + gpy:4 + gpy],
                      in1=x2b[gp][:, sl], op0=ALU.mult, op1=ALU.add,
                      deps=[tokB["o"], tokB["ydma"][gpy], tokB["x2w"][gp]])
                    tokB["epi"][hh] = E("vector", "tensor_tensor",
                                        out=outb[gpy][:, sl], in0=outb[gpy][:, sl],
                                        in1=b2b[:, sl], op=ALU.add)
                tokB["x2rd"][gp] = tokB["epi"][1]
                tokB["ydma"][gpy] = E("sync", "dma_start",
                                      out=y_d[mp * 128:(mp + 1) * 128, :],
                                      in_=outb[gpy][:], deps=[tokB["epi"][1]])
            emit_prefetch(m + 2)
            emit_front(m + 1)

        esB.close()
        E.final_wait("sync")
        E.flush(block)
    return nc


def shard_inputs(x, Wq, bq, Wk, bk, Wv, bv, gamma, W1, b1, W2, b2,
                 n_span=4, n_warm=4):
    f32 = np.float32
    B, S, D_ = x.shape
    span = S // n_span
    W = n_warm * C
    x = np.ascontiguousarray(np.asarray(x, dtype=f32))
    common = {
        "wqt": np.ascontiguousarray(np.asarray(Wq, f32).T),
        "wkt": np.ascontiguousarray(np.asarray(Wk, f32).T),
        "wvt": np.ascontiguousarray(np.asarray(Wv, f32).T),
        "bq": np.asarray(bq, f32).reshape(1, -1),
        "bk": np.asarray(bk, f32).reshape(1, -1),
        "bv": np.asarray(bv, f32).reshape(1, -1),
        "gam": np.asarray(gamma, f32).reshape(1, 1),
        "w1t": np.ascontiguousarray(np.asarray(W1, f32).T),
        "b1": np.asarray(b1, f32).reshape(1, -1),
        "w2t": np.ascontiguousarray(np.asarray(W2, f32).T),
        "b2": np.asarray(b2, f32).reshape(1, -1),
    }
    in_maps = []
    for core in range(B * n_span):
        b = core // n_span
        s = core % n_span
        if s == 0:
            xs = np.concatenate([np.zeros((W, D_), f32), x[b, 0:span]], axis=0)
            wm = np.zeros((128, 1), f32)
        else:
            xs = x[b, s * span - W:(s + 1) * span]
            wm = np.ones((128, 1), f32)
        mm = dict(common)
        mm["x"] = np.ascontiguousarray(xs)
        mm["wmask"] = wm
        in_maps.append(mm)
    return in_maps, span


def kernel(L0_attn_x_seq, L0_attn_Wq, L0_attn_bq, L0_attn_Wk, L0_attn_bk,
           L0_attn_Wv, L0_attn_bv, L0_attn_gamma, L0_ffn_W1, L0_ffn_b1,
           L0_ffn_W2, L0_ffn_b2):
    n_span, n_warm = 4, 3
    B, S, D_ = L0_attn_x_seq.shape
    in_maps, span = shard_inputs(
        L0_attn_x_seq, L0_attn_Wq, L0_attn_bq, L0_attn_Wk, L0_attn_bk,
        L0_attn_Wv, L0_attn_bv, L0_attn_gamma, L0_ffn_W1, L0_ffn_b1,
        L0_ffn_W2, L0_ffn_b2, n_span=n_span, n_warm=n_warm)
    nc = build_nc(n_main=span // C, n_warm=n_warm)
    res = run_bass_kernel_spmd(nc, in_maps, core_ids=list(range(8)))
    out = np.empty((B, S, D_), dtype=np.float32)
    for core in range(8):
        b = core // n_span
        s = core % n_span
        out[b, s * span:(s + 1) * span] = res.results[core]["y"]
    return out


# version 37 (final)
# speedup vs baseline: 1.1003x; 1.0113x over previous
import numpy as np
from contextlib import ExitStack

import concourse.bass as bass
import concourse.mybir as mybir
from concourse.bass_utils import run_bass_kernel_spmd

F32 = mybir.dt.float32
F16 = mybir.dt.float16
BF16 = mybir.dt.bfloat16
FP8 = mybir.dt.float8e4
I32 = mybir.dt.int32
AF = mybir.ActivationFunctionType
ALU = mybir.AluOpType
AX = mybir.AxisListType

D = 1024
H = 4096
C = 128
KT = D // 128
EPS_LN = 1e-5
EPS_DEN = 1e-6
EPS_Q = 1e-5
QB = 127.0

ENGS = ("tensor", "vector", "scalar", "gpsimd", "sync")
SEMS = ENGS + ("gdma",)


class Em:

    def __init__(self, nc, es):
        self.nc = nc
        self.sem = {n: es.enter_context(nc.semaphore("S_" + n)) for n in SEMS}
        self.cnt = {n: 0 for n in SEMS}
        self.seen = {a: {b: 0 for b in SEMS} for a in SEMS}
        self.ops = {n: [] for n in ENGS}
        self.strict_scalar = True

    def __call__(self, eng, meth, *args, deps=(), inc=None, **kw):
        sem = eng
        if eng == "gdma":
            eng = "gpsimd"
        dl = [d for d in deps if d is not None]
        if sem in ("vector", "scalar", "gpsimd") and self.cnt[sem] > 0:
            dl.append((sem, self.cnt[sem]))
        if sem in ("sync", "gdma") and self.cnt[sem] > 0:
            dl.append((sem, self.cnt[sem]))
        if (self.strict_scalar and sem in ("scalar", "gpsimd")
                and self.cnt["vector"] > 0):
            dl.append(("vector", self.cnt["vector"]))
        waits = []
        for dn, dv in dl:
            if dv > self.seen[eng][dn]:
                waits.append((dn, dv))
                self.seen[eng][dn] = dv
        if inc is None:
            inc = 16 if meth.startswith("dma") else (1 if eng != "tensor" else 0)
        self.ops[eng].append((meth, args, kw, waits, inc, sem))
        if inc:
            self.cnt[sem] += inc
            return (sem, self.cnt[sem])
        return None

    def barrier_tokens(self):
        return [(n, self.cnt[n]) for n in SEMS]

    def final_wait(self, eng):
        self.ops[eng].append(("__wait__", (), {},
                              [(n, self.cnt[n]) for n in ("sync", "gdma")], 0,
                              eng))

    def flush(self, block):
        for name in ENGS:
            ops = self.ops[name]

            def mk(ops=ops):
                def run(h):
                    for meth, args, kw, waits, inc, sem in ops:
                        for dn, dv in waits:
                            h.wait_ge(self.sem[dn], dv)
                        if meth == "__wait__":
                            continue
                        r = getattr(h, meth)(*args, **kw)
                        if inc:
                            r.then_inc(self.sem[sem], inc)
                return run
            getattr(block, name)(mk())


def build_nc(n_main=32, n_warm=4):
    nc = bass.Bass("TRN2", target_bir_lowering=False)
    n_tot = n_main + n_warm
    n_tok_in = n_tot * C
    n_tok_out = n_main * C

    x_d = nc.dram_tensor("x", [n_tok_in, D], F32, kind="ExternalInput").ap()
    wqt_d = nc.dram_tensor("wqt", [D, D], F32, kind="ExternalInput").ap()
    wkt_d = nc.dram_tensor("wkt", [D, D], F32, kind="ExternalInput").ap()
    wvt_d = nc.dram_tensor("wvt", [D, D], F32, kind="ExternalInput").ap()
    bq_d = nc.dram_tensor("bq", [1, D], F32, kind="ExternalInput").ap()
    bk_d = nc.dram_tensor("bk", [1, D], F32, kind="ExternalInput").ap()
    bv_d = nc.dram_tensor("bv", [1, D], F32, kind="ExternalInput").ap()
    gam_d = nc.dram_tensor("gam", [1, 1], F32, kind="ExternalInput").ap()
    wmask_d = nc.dram_tensor("wmask", [128, 1], F32, kind="ExternalInput").ap()
    w1t_d = nc.dram_tensor("w1t", [D, H], F32, kind="ExternalInput").ap()
    b1_d = nc.dram_tensor("b1", [1, H], F32, kind="ExternalInput").ap()
    w2t_d = nc.dram_tensor("w2t", [H, D], F32, kind="ExternalInput").ap()
    b2_d = nc.dram_tensor("b2", [1, D], F32, kind="ExternalInput").ap()
    y_d = nc.dram_tensor("y", [n_tok_out, D], F32, kind="ExternalOutput").ap()
    x2s_d = nc.dram_tensor("x2scr", [n_tok_out, D], F32).ap()
    xqs_d = nc.dram_tensor("xqscr", [n_tok_out, D], BF16).ap()
    w1q_d = nc.dram_tensor("w1q", [D, H], FP8).ap()
    w2q_d = nc.dram_tensor("w2q", [H, D], FP8).ap()

    es = ExitStack()
    with es:
        block = es.enter_context(nc.Block())
        E = Em(nc, es)

        psb = lambda name, shape, dt=F32: es.enter_context(nc.sbuf_tensor(name, shape, dt))
        onesrow = psb("onesrow", [1, 128])
        onescol = psb("onescol", [128, 1])
        gvec = psb("gvec", [128, 1])
        givec = psb("givec", [128, 1])
        gcvec = psb("gcvec", [128, 1])
        maskvec = psb("maskvec", [128, 1])
        ws1b = psb("ws1b", [128, 1])
        hws1b = psb("hws1b", [128, 2])
        ws2b = psb("ws2b", [128, 1])
        hws2b = psb("hws2b", [128, 2])
        w2s127 = psb("w2s127", [128, 1])
        iden = psb("iden", [128, 128])
        idenb = psb("idenb", [128, 128], BF16)
        maskUT = psb("maskUT", [128, 128])
        gamsc = psb("gamsc", [1, 1])
        epsv = psb("epsv", [128, 1])
        stats = psb("stats", [128, 3 * n_main])
        r1 = psb("r1", [128, 16])
        r2 = psb("r2", [128, 16])

        esA = ExitStack()
        pA = esA.enter_context(nc.psum_tensor("pA", [128, 1024], F32))
        pB = esA.enter_context(nc.psum_tensor("pB", [128, 1024], F32))
        pC = esA.enter_context(nc.psum_tensor("pC", [128, 1024], F32))
        pD = esA.enter_context(nc.psum_tensor("pD", [128, 1024], F32))

        gbq = esA.enter_context(nc.sbuf_tensor("gbq", [128, D], F32))
        gbk = esA.enter_context(nc.sbuf_tensor("gbk", [128, D], F32))
        bvb = esA.enter_context(nc.sbuf_tensor("bvb", [128, D], F32))

        esI = ExitStack()
        isb = lambda name, shape, dt=F32: esI.enter_context(nc.sbuf_tensor(name, shape, dt))
        gamrow = isb("gamrow", [1, 128])
        zerrow = isb("zerrow", [1, 128])
        grow = isb("grow", [1, 128])
        girow = isb("girow", [1, 128])
        iotat = isb("iotat", [128, 128])
        bqrow = isb("bqrow", [1, D])
        bkrow = isb("bkrow", [1, D])
        bvrow = isb("bvrow", [1, D])

        t_in0 = E("sync", "dma_start", out=gamsc[:], in_=gam_d[:])
        E("sync", "dma_start", out=bqrow[:], in_=bq_d[:])
        E("sync", "dma_start", out=bkrow[:], in_=bk_d[:])
        t_rows = E("sync", "dma_start", out=bvrow[:], in_=bv_d[:])
        t_in1 = E("sync", "dma_start", out=maskvec[:], in_=wmask_d[:])

        tg_io = E("gpsimd", "iota", out=iotat[:], pattern=[[1, 128]], base=0,
                  channel_multiplier=-1, allow_small_or_imprecise_dtypes=True)

        E("vector", "memset", onesrow[:], 1.0)
        E("vector", "memset", onescol[:], 1.0)
        E("vector", "memset", zerrow[:], 0.0)
        E("vector", "memset", epsv[:], EPS_LN)
        E("vector", "tensor_scalar", out=maskUT[:], in0=iotat[:], scalar1=0.0,
          scalar2=None, op0=ALU.is_ge, deps=[tg_io])
        E("vector", "tensor_scalar", out=iden[:], in0=iotat[:], scalar1=0.0,
          scalar2=None, op0=ALU.is_equal)
        E("vector", "tensor_copy", out=idenb[:], in_=iden[:])
        E("vector", "memset", gamrow[:], 1.0)
        E("vector", "tensor_scalar", out=gamrow[:], in0=gamrow[:],
          scalar1=gamsc[0:1, 0:1], scalar2=None, op0=ALU.mult, deps=[t_in0])
        E("vector", "tensor_tensor_scan", out=grow[:], data0=gamrow[:],
          data1=zerrow[:], initial=1.0, op0=ALU.mult, op1=ALU.add)
        E("vector", "reciprocal", out=r1[0:1, 0:1], in_=gamsc[0:1, 0:1])
        E("vector", "memset", gamrow[:], 1.0)
        E("vector", "tensor_scalar", out=gamrow[:], in0=gamrow[:],
          scalar1=r1[0:1, 0:1], scalar2=None, op0=ALU.mult)
        tv_rows = E("vector", "tensor_tensor_scan", out=girow[:], data0=gamrow[:],
                    data1=zerrow[:], initial=1.0, op0=ALU.mult, op1=ALU.add)

        tvgci = E("vector", "tensor_scalar", out=gamrow[:], in0=girow[:],
                  scalar1=grow[0:1, 127:128], scalar2=None, op0=ALU.mult)
        E("tensor", "transpose", pD[:, 0:1], grow[:], iden[0:1, 0:1],
          deps=[tv_rows, tvgci])
        E("tensor", "transpose", pD[:, 128:129], gamrow[:], iden[0:1, 0:1])
        tt_g = E("tensor", "matmul", pD[:, 256:257], onesrow[:],
                 grow[0:1, 127:128], start=True, stop=True, inc=1)
        E("vector", "tensor_copy", out=gvec[:], in_=pD[:, 0:1], deps=[tt_g])
        E("vector", "tensor_copy", out=givec[:], in_=pD[:, 128:129])
        tv_g = E("vector", "tensor_copy", out=gcvec[:], in_=pD[:, 256:257])
        E("vector", "reciprocal", out=r1[:, 15:16], in_=gcvec[:])
        E("vector", "tensor_scalar", out=maskUT[:], in0=maskUT[:],
          scalar1=r1[:, 15:16], scalar2=None, op0=ALU.mult)

        for hh in range(2):
            sl = slice(hh * 512, (hh + 1) * 512)
            E("tensor", "matmul", pA[:, sl], grow[:], bqrow[:, sl], start=True,
              stop=True, deps=[tv_g, t_rows])
            E("tensor", "matmul", pB[:, sl], gamrow[:], bkrow[:, sl], start=True, stop=True)
            tt_b = E("tensor", "matmul", pC[:, sl], onesrow[:], bvrow[:, sl],
                     start=True, stop=True, inc=1)
        E("vector", "tensor_copy", out=gbq[:], in_=pA[:], deps=[tt_b])
        E("vector", "tensor_copy", out=gbk[:], in_=pB[:])
        tv_bias = E("vector", "tensor_copy", out=bvb[:], in_=pC[:])
        barrI = E.barrier_tokens()
        esI.close()

        sbA = lambda name, shape, dt=F32: esA.enter_context(nc.sbuf_tensor(name, shape, dt))
        WQTH = sbA("WQTH", [128, KT * D], F16)
        WQTL = sbA("WQTL", [128, KT * D], F16)
        WKTH = sbA("WKTH", [128, KT * D], F16)
        WKTL = sbA("WKTL", [128, KT * D], F16)
        WVTH = sbA("WVTH", [128, KT * D], F16)
        WVTL = sbA("WVTL", [128, KT * D], F16)
        S_sb = sbA("S_sb", [128, KT * D])
        z_sb = sbA("z_sb", [128, KT])
        xin = [sbA("xin0", [128, D]), sbA("xin1", [128, D])]
        bufA = sbA("bufA", [128, D])
        bufB = sbA("bufB", [128, D])
        dmp2 = sbA("dmp2", [128, D])
        xhatTh = sbA("xhatTh", [128, D], F16)
        xhatTl = sbA("xhatTl", [128, D], F16)
        qt = sbA("qt", [128, D])
        kt_ = sbA("kt", [128, D])
        kth16 = sbA("kth16", [128, D], F16)
        ktl16 = sbA("ktl16", [128, D], F16)
        vt = sbA("vt", [128, D])
        vh16 = sbA("vh16", [128, D], F16)
        vl16 = sbA("vl16", [128, D], F16)
        qf = sbA("qf", [128, D])
        kf = sbA("kf", [128, D])
        pmask = sbA("pmask", [128, 128])
        pmh = sbA("pmh", [128, 128], F16)
        pml = sbA("pml", [128, 128], F16)
        ones16 = sbA("ones16", [128, 1], F16)
        wstg = sbA("wstg", [128, 1024])
        wg1 = sbA("wg1", [128, 512], FP8)
        wg2 = sbA("wg2", [128, 512], FP8)
        absacc = sbA("absacc", [128, 64])
        wqr = sbA("wqr", [128, 4])
        tqA = esA.enter_context(nc.sbuf_tensor("tqA", [128, D], I32))
        xqA = sbA("xqA", [128, D], BF16)

        E("vector", "memset", S_sb[:], 0.0, deps=barrI)
        tv_z0 = E("vector", "memset", z_sb[:], 0.0)
        t_x0 = E("gdma", "dma_start", out=xin[0][:], in_=x_d[0:128, :], deps=barrI)
        E("vector", "tensor_copy", out=ones16[:], in_=onescol[:], deps=barrI)
        wsplit = {"rd": [None, None]}
        def load_w_pair(dram, WH, WL):
            last = None
            for kt in range(KT):
                h = kt % 2
                buf = wstg if h == 0 else qt
                td = E("sync" if h == 0 else "gdma", "dma_start", out=buf[:],
                       in_=dram[kt * 128:(kt + 1) * 128, :],
                       deps=[wsplit["rd"][h]] + barrI)
                sl = slice(kt * 1024, (kt + 1) * 1024)
                E("vector", "tensor_copy", out=WH[:, sl], in_=buf[:], deps=[td])
                last = E("vector", "tensor_tensor", out=WL[:, sl], in0=buf[:],
                         in1=WH[:, sl], op=ALU.subtract)
                wsplit["rd"][h] = last
            return last
        t_wk = load_w_pair(wkt_d, WKTH, WKTL)
        t_wv = load_w_pair(wvt_d, WVTH, WVTL)
        t_wq = load_w_pair(wqt_d, WQTH, WQTL)

        tok = {
            "xq_ev": None, "xq_odd": None,
            "v15": tv_bias, "v16": None, "v17": None, "v21": tv_z0,
            "num": None, "den": None, "P": None,
            "q": None, "k": None, "v": t_wv,
            "trq": None, "trk": None, "trx": None,
            "L": [None] * KT, "z": None,
            "v19": None, "s5": None, "x2dma": None,
            "v9": None, "v10": None, "v11": None, "v12": None, "v13": None,
            "v14": tv_bias, "s3": None, "s4": None, "pDfree": tv_bias,
            "v20": [tv_z0] * KT, "v20last": tv_z0,
            "xin_rd": [[t_x0], []],
            "v8": None, "s1": None, "tskh": None, "xqdma": None,
        }

        def ln1(c, dep_dma):
            g = c % 2
            E("scalar", "activation", out=dmp2[:], in_=xin[g][:],
              func=AF.Copy, bias=0.0, scale=1.0,
              accum_out=r1[:, 0:1], deps=[dep_dma, tok["s5"]])
            tok["s1"] = E("scalar", "activation", out=bufB[:], in_=xin[g][:],
                          func=AF.Square, bias=0.0, scale=1.0,
                          accum_out=r1[:, 1:2], deps=[dep_dma, tok["v11"],
                                                      tok["trk"]])
            E("vector", "tensor_scalar", out=r1[:, 2:3], in0=r1[:, 0:1],
              scalar1=1.0 / D, scalar2=None, op0=ALU.mult, deps=[tok["s1"]])
            E("vector", "tensor_scalar", out=r1[:, 3:4], in0=r1[:, 1:2],
              scalar1=1.0 / D, scalar2=None, op0=ALU.mult, deps=[tok["s1"]])
            E("vector", "tensor_scalar", out=r1[:, 4:5], in0=r1[:, 2:3],
              scalar1=r1[:, 2:3], scalar2=None, op0=ALU.mult)
            tv = E("vector", "tensor_tensor", out=r1[:, 5:6], in0=r1[:, 3:4],
                   in1=r1[:, 4:5], op=ALU.subtract)
            ts = E("scalar", "activation", out=r1[:, 6:7], in_=r1[:, 5:6],
                   func=AF.Sqrt, bias=epsv[:, 0:1], scale=1.0, deps=[tv])
            E("vector", "reciprocal", out=r1[:, 7:8], in_=r1[:, 6:7], deps=[ts])
            E("vector", "tensor_scalar", out=r1[:, 8:9], in0=r1[:, 2:3],
              scalar1=r1[:, 7:8], scalar2=-1.0, op0=ALU.mult, op1=ALU.mult)
            tok["v8"] = E("vector", "tensor_scalar", out=bufA[:], in0=xin[g][:],
                          scalar1=r1[:, 7:8], scalar2=r1[:, 8:9], op0=ALU.mult,
                          op1=ALU.add, deps=[tok["trq"], tok["trx"]])
            tok["xin_rd"][g] = [tok["v8"]]

        def front(c):
            for dt_ in range(KT):
                tok_t = E("tensor", "transpose", pD[:, dt_ * 128:(dt_ + 1) * 128],
                          bufA[:, dt_ * 128:(dt_ + 1) * 128], iden[:],
                          deps=[tok["v8"], tok["pDfree"]] if dt_ == 0 else (),
                          inc=1 if dt_ == KT - 1 else 0)
            tok["trx"] = tok_t
            tsh = E("scalar", "copy", out=xhatTh[:], in_=pD[:],
                    deps=[tok["trx"], tok["v"]])
            tok["v9"] = E("vector", "tensor_tensor", out=xhatTl[:], in0=pD[:],
                          in1=xhatTh[:], op=ALU.subtract, deps=[tsh])
            tok["pDfree"] = tok["v9"]

        def ln2_stats(c):
            m = c - n_warm
            g = c % 2
            E("scalar", "activation", out=dmp2[:], in_=xin[g][:],
              func=AF.Copy, bias=0.0, scale=1.0,
              accum_out=r2[:, 2:3], deps=[tok["v19"]])
            tok["s5"] = E("scalar", "activation", out=dmp2[:], in_=xin[g][:],
                          func=AF.Square, bias=0.0, scale=1.0,
                          accum_out=r2[:, 3:4], deps=[tok["v19"]])
            E("vector", "tensor_reduce", out=r2[:, 4:5], in_=xin[g][:],
              axis=AX.X, op=ALU.max)
            E("vector", "tensor_reduce", out=r2[:, 5:6], in_=xin[g][:],
              axis=AX.X, op=ALU.min, negate=True)
            E("vector", "tensor_scalar", out=r2[:, 6:7], in0=r2[:, 2:3],
              scalar1=1.0 / D, scalar2=None, op0=ALU.mult, deps=[tok["s5"]])
            E("vector", "tensor_scalar", out=r2[:, 7:8], in0=r2[:, 3:4],
              scalar1=1.0 / D, scalar2=None, op0=ALU.mult, deps=[tok["s5"]])
            E("vector", "tensor_scalar", out=r2[:, 8:9], in0=r2[:, 6:7],
              scalar1=r2[:, 6:7], scalar2=None, op0=ALU.mult)
            tvv = E("vector", "tensor_tensor", out=r2[:, 9:10], in0=r2[:, 7:8],
                    in1=r2[:, 8:9], op=ALU.subtract)
            tss = E("scalar", "activation", out=r2[:, 10:11], in_=r2[:, 9:10],
                    func=AF.Sqrt, bias=epsv[:, 0:1], scale=1.0, deps=[tvv])
            E("vector", "reciprocal", out=r2[:, 11:12], in_=r2[:, 10:11],
              deps=[tss])
            E("vector", "tensor_tensor", out=r2[:, 12:13], in0=r2[:, 4:5],
              in1=r2[:, 6:7], op=ALU.subtract)
            E("vector", "tensor_tensor", out=r2[:, 13:14], in0=r2[:, 5:6],
              in1=r2[:, 6:7], op=ALU.add)
            E("vector", "tensor_tensor", out=r2[:, 12:13], in0=r2[:, 12:13],
              in1=r2[:, 13:14], op=ALU.max)
            E("vector", "tensor_scalar", out=r2[:, 12:13], in0=r2[:, 12:13],
              scalar1=r2[:, 11:12], scalar2=EPS_Q, op0=ALU.mult, op1=ALU.max)
            E("vector", "reciprocal", out=r2[:, 14:15], in_=r2[:, 12:13])
            E("vector", "tensor_scalar", out=r2[:, 14:15], in0=r2[:, 14:15],
              scalar1=QB, scalar2=None, op0=ALU.mult)
            E("vector", "tensor_scalar", out=stats[:, 3 * m:3 * m + 1],
              in0=r2[:, 11:12], scalar1=r2[:, 14:15], scalar2=None,
              op0=ALU.mult)
            E("vector", "tensor_scalar", out=stats[:, 3 * m + 1:3 * m + 2],
              in0=r2[:, 6:7], scalar1=stats[:, 3 * m:3 * m + 1], scalar2=-1.0,
              op0=ALU.mult, op1=ALU.mult)
            tvs = E("vector", "tensor_scalar",
                    out=stats[:, 3 * m + 2:3 * m + 3],
                    in0=r2[:, 12:13], scalar1=1.0 / QB, scalar2=None,
                    op0=ALU.mult)
            sca = stats[:, 3 * m:3 * m + 1]
            scb = stats[:, 3 * m + 1:3 * m + 2]
            sv = E("scalar", "activation", out=vt[:], in_=xin[g][:],
                   func=AF.Identity, scale=sca, bias=scb, deps=[tvs])
            st = E("scalar", "activation", out=tqA[:], in_=xin[g][:],
                   func=AF.Identity, scale=sca, bias=scb)
            E("vector", "scalar_tensor_tensor", out=qt[:], in0=tqA[:],
              scalar=0.5, in1=vt[:], op0=ALU.add, op1=ALU.is_le,
              deps=[sv, st, wq_state["rd"][1]])
            E("vector", "scalar_tensor_tensor", out=kt_[:], in0=tqA[:],
              scalar=-0.5, in1=vt[:], op0=ALU.add, op1=ALU.is_ge,
              deps=[tok["tskh"]])
            tst2 = E("vector", "scalar_tensor_tensor", out=vt[:], in0=qt[:],
                     scalar=1.0, in1=tqA[:], op0=ALU.mult, op1=ALU.add)
            wq_state["rd"][1] = tst2
            txq = E("vector", "tensor_tensor", out=xqA[:], in0=vt[:],
                    in1=kt_[:], op=ALU.subtract, deps=[tok["xqdma"]])
            tok["xqdma"] = E("gdma", "dma_start",
                             out=xqs_d[m * 128:(m + 1) * 128, :],
                             in_=xqA[:], deps=[txq])
            tok["xin_rd"][g] = [tvs, st, tok["x2dma"]]

        def wq_sub(pt):
            if wq_state.get("ws_pending") and pt == 0:
                wq_state["ws_pending"] = False
                wq_ws_compute()
            kindw, parts = wq_state["mid"]
            if kindw is None or pt >= len(parts):
                return
            items = parts[pt]
            if not items:
                return
            if kindw == "p1":
                wq_pass1_slice(items)
            else:
                wq_pass2_slice(items)

        def chunk(c):
            warm = c < n_warm
            m = c - n_warm
            g = c % 2
            if c + 1 < n_tot:
                gn = (c + 1) % 2
                tok_dma_next = E("gdma", "dma_start", out=xin[gn][:],
                                 in_=x_d[(c + 1) * 128:(c + 2) * 128, :],
                                 deps=tok["xin_rd"][gn])
            else:
                tok_dma_next = None

            def proj(WH, WL, wtok, pdst, dep_free):
                t = None
                for hh in range(2):
                    for kk in range(KT):
                        co = kk * D + hh * 512
                        for ph, (lh, rh) in enumerate(
                                ((xhatTh, WH), (xhatTl, WH), (xhatTh, WL))):
                            t = E("tensor", "matmul",
                                  pdst[:, hh * 512:(hh + 1) * 512],
                                  lh[:, kk * 128:(kk + 1) * 128],
                                  rh[:, co:co + 512],
                                  start=(kk == 0 and ph == 0),
                                  stop=(kk == KT - 1 and ph == 2),
                                  deps=([tok["v9"], wtok] + dep_free)
                                  if (hh == 0 and kk == 0 and ph == 0) else (),
                                  inc=1 if (hh == 1 and kk == KT - 1 and ph == 2) else 0)
                return t

            if not warm:
                tok["q"] = proj(WQTH, WQTL, t_wq, pA, [tok["xq_ev"], tok["v19"]])
                tok["v10"] = E("vector", "scalar_tensor_tensor", out=bufA[:],
                               in0=pA[:], scalar=gvec[:, 0:1], in1=gbq[:],
                               op0=ALU.mult, op1=ALU.add,
                               deps=[tok["q"], tok["trx"]])
            tok["k"] = proj(WKTH, WKTL, t_wk, pB, [tok["v16"], tok["v21"]])
            tok["v11"] = E("vector", "scalar_tensor_tensor", out=bufB[:],
                           in0=pB[:], scalar=givec[:, 0:1], in1=gbk[:],
                           op0=ALU.mult, op1=ALU.add,
                           deps=[tok["k"], tok["s1"]])
            tok["s4"] = E("scalar", "activation", out=kt_[:], in_=bufB[:],
                          func=AF.Relu, bias=0.0, scale=1.0,
                          deps=[tok["v11"]])
            tok["v"] = proj(WVTH, WVTL, t_wv, pC, [tok["xq_odd"], tok["v12"]])
            tvt = E("vector", "tensor_tensor", out=vt[:], in0=pC[:],
                    in1=bvb[:], op=ALU.add, deps=[tok["v"], tok["L"][KT - 1]])
            if not warm:
                for dt_ in range(KT):
                    tok_t = E("tensor", "transpose", pD[:, dt_ * 128:(dt_ + 1) * 128],
                              bufA[:, dt_ * 128:(dt_ + 1) * 128], iden[:],
                              deps=[tok["v10"], tok["pDfree"], tok["den"]]
                              if dt_ == 0 else (),
                              inc=1 if dt_ == KT - 1 else 0)
                tok["trq"] = tok_t
                tok["v14"] = E("vector", "tensor_scalar", out=qf[:], in0=pD[:],
                               scalar1=0.0, scalar2=None, op0=ALU.max,
                               deps=[tok["trq"]])
                tok["pDfree"] = tok["v14"]
                for dt_ in range(KT):
                    tok_t = E("tensor", "transpose", pB[:, dt_ * 128:(dt_ + 1) * 128],
                              bufB[:, dt_ * 128:(dt_ + 1) * 128], iden[:],
                              deps=[tok["v11"]] if dt_ == 0 else (),
                              inc=1 if dt_ == KT - 1 else 0)
                tok["trk"] = tok_t
                tok["v15"] = E("scalar", "activation", out=kf[:], in_=pB[:, 0:1024],
                               func=AF.Relu, bias=0.0, scale=1.0,
                               deps=[tok["trk"], tok["P"]])
            tskh = E("scalar", "copy", out=kth16[:], in_=kt_[:], deps=[tok["z"]])
            tok["tskh"] = tskh
            tsvh = E("scalar", "copy", out=vh16[:], in_=vt[:],
                     deps=[tok["L"][KT - 1]])
            if not warm:
                for kk in range(KT):
                    tok_t = E("tensor", "matmul", pB[:, 0:128],
                              kf[:, kk * 128:(kk + 1) * 128],
                              qf[:, kk * 128:(kk + 1) * 128],
                              start=(kk == 0), stop=(kk == KT - 1),
                              deps=[tok["v15"], tok["v14"], tok["v11"]] if kk == 0 else (),
                              inc=1 if kk == KT - 1 else 0)
                tok["P"] = tok_t
                E("vector", "tensor_tensor", out=pmask[:],
                  in0=pB[:, 0:128], in1=maskUT[:], op=ALU.mult,
                  deps=[tok["P"]])
                tsph = E("scalar", "copy", out=pmh[:], in_=pmask[:])
                tok["v16"] = E("vector", "tensor_tensor", out=pml[:],
                               in0=pmask[:], in1=pmh[:], op=ALU.subtract,
                               deps=[tsph])
            tok["v13"] = E("vector", "tensor_tensor", out=ktl16[:], in0=kt_[:],
                           in1=kth16[:], op=ALU.subtract,
                           deps=[tskh, tok["z"]])
            tok["v12"] = E("vector", "tensor_tensor", out=vl16[:], in0=vt[:],
                           in1=vh16[:], op=ALU.subtract, deps=[tsvh])
            if c + 1 < n_tot:
                ln1(c + 1, tok_dma_next)
            wq_sub(0)

            if not warm:
                for hh in range(2):
                    sl = slice(hh * 512, (hh + 1) * 512)
                    for it in range(KT):
                        E("tensor", "matmul", pA[:, sl],
                          qf[:, it * 128:(it + 1) * 128],
                          S_sb[:, it * D + hh * 512: it * D + hh * 512 + 512],
                          start=(it == 0), stop=False,
                          deps=[tok["v14"], tok["v20last"], tok["v10"]]
                          if (hh == 0 and it == 0) else (), inc=0)
                    for ph, (lh, rh) in enumerate(
                            ((pmh, vh16), (pml, vh16), (pmh, vl16))):
                        tok_t = E("tensor", "matmul", pA[:, sl], lh[:], rh[:, sl],
                                  start=False, stop=(ph == 2),
                                  deps=[tok["v16"], tok["v12"]]
                                  if (hh == 0 and ph == 0) else (),
                                  inc=1 if (hh == 1 and ph == 2) else 0)
                tok["num"] = tok_t
                for it in range(KT):
                    E("tensor", "matmul", pB[:, 512:513],
                      qf[:, it * 128:(it + 1) * 128], z_sb[:, it:it + 1],
                      start=(it == 0), stop=False,
                      deps=[tok["v21"], tok["v17"], tok["v15"]] if it == 0 else (),
                      inc=0)
                E("tensor", "matmul", pB[:, 512:513], pmh[:], ones16[:],
                  start=False, stop=False)
                tok["den"] = E("tensor", "matmul", pB[:, 512:513], pml[:],
                               ones16[:], start=False, stop=True, inc=1)
                tok["v17"] = E("vector", "tensor_scalar", out=r2[:, 0:1],
                               in0=pB[:, 512:513], scalar1=EPS_DEN, scalar2=None,
                               op0=ALU.add, deps=[tok["den"]])
                E("vector", "reciprocal", out=r2[:, 1:2], in_=r2[:, 0:1])
                tok["v19"] = E("vector", "scalar_tensor_tensor", out=xin[g][:],
                               in0=pA[:], scalar=r2[:, 1:2], in1=xin[g][:],
                               op0=ALU.mult, op1=ALU.add,
                               deps=[tok["num"], tok["s5"], tok["x2dma"]])
                tok["x2dma"] = E("gdma", "dma_start",
                                 out=x2s_d[m * 128:(m + 1) * 128, :],
                                 in_=xin[g][:], deps=[tok["v19"]])

            if c + 1 < n_tot:
                front(c + 1)
            wq_sub(1)

            for it in range(KT):
                pdst = pA if it % 2 == 0 else pC
                if it == 0:
                    dfree = [tok["v13"], tok["v12"], tok["xq_ev"],
                             (tok["v19"] if not warm else None)]
                elif it == 1:
                    dfree = [tok["xq_odd"]]
                else:
                    dfree = [tok["v20"][it - 2]]
                for hh in range(2):
                    sl = slice(hh * 512, (hh + 1) * 512)
                    ksl = slice(it * 128, (it + 1) * 128)
                    for ph, (lh, rh) in enumerate(
                            ((kth16, vh16), (ktl16, vh16), (kth16, vl16))):
                        tok_t = E("tensor", "matmul", pdst[:, sl],
                                  lh[:, ksl], rh[:, sl],
                                  start=(ph == 0), stop=(ph == 2),
                                  deps=dfree if (hh == 0 and ph == 0) else (),
                                  inc=1 if (hh == 1 and ph == 2) else 0)
                tok["L"][it] = tok_t
                tok["v20"][it] = E("vector", "scalar_tensor_tensor",
                                   out=S_sb[:, it * D:(it + 1) * D],
                                   in0=S_sb[:, it * D:(it + 1) * D],
                                   scalar=gcvec[:, 0:1], in1=pdst[:],
                                   op0=ALU.mult, op1=ALU.add,
                                   deps=[tok["L"][it], tok["num"]])
            tok["v20last"] = tok["v20"][KT - 1]
            tok["xq_ev"] = tok["v20"][KT - 2]
            tok["xq_odd"] = tok["v20"][KT - 1]

            wq_sub(2)

            for it in range(KT):
                ksl = slice(it * 128, (it + 1) * 128)
                for ph, lh in enumerate((kth16, ktl16)):
                    tok_t = E("tensor", "matmul", pB[:, 520 + it:521 + it],
                              lh[:, ksl], ones16[:],
                              start=(it == 0 and ph == 0),
                              stop=(it == KT - 1 and ph == 1),
                              deps=[tok["v17"], tok["v13"], tok["v11"]]
                              if (it == 0 and ph == 0) else (),
                              inc=1 if (it == KT - 1 and ph == 1) else 0)
            tok["z"] = tok_t
            tok["v21"] = E("vector", "scalar_tensor_tensor", out=z_sb[:],
                           in0=z_sb[:], scalar=gcvec[:, 0:1], in1=pB[:, 520:528],
                           op0=ALU.mult, op1=ALU.add,
                           deps=[tok["z"], tok["den"]])
            wq_sub(3)

            if warm and c == n_warm - 1:
                E("vector", "tensor_scalar", out=S_sb[:], in0=S_sb[:],
                  scalar1=maskvec[:, 0:1], scalar2=None, op0=ALU.mult,
                  deps=[tok["v20last"], t_in1])
                tok["v21"] = E("vector", "tensor_scalar", out=z_sb[:], in0=z_sb[:],
                               scalar1=maskvec[:, 0:1], scalar2=None, op0=ALU.mult)
                tok["v20last"] = tok["v21"]
                tok["xq_ev"] = tok["v21"]
                tok["xq_odd"] = tok["v21"]



        def w_subtiles(which):
            if which == 1:
                nj = H // 1024
                for i in range(KT):
                    for j in range(nj):
                        yield (w1t_d[i * 128:(i + 1) * 128, j * 1024:(j + 1) * 1024],
                               i * nj + j)
            else:
                for i in range(H // 128):
                    yield w2t_d[i * 128:(i + 1) * 128, :], i

        n1 = KT * (H // 1024)
        n2 = H // 128
        subs = ([(ap, 0, idx) for ap, idx in w_subtiles(1)]
                + [(ap, n1, idx) for ap, idx in w_subtiles(2)])
        nsub = len(subs)
        wq_state = {"rd": [wsplit["rd"][0], wsplit["rd"][1]], "n": 0, "w8dma": None,
                    "wg_n": 0, "w8": [None, None],
                    "wsb": None, "abslast": None, "mid": (None, ())}

        def wq_pass1_slice(items):
            for ap, base, idx in items:
                b = wq_state["n"] % 2
                wq_state["n"] += 1
                buf = wstg if b == 0 else qt
                td = E("sync", "dma_start", out=buf[:],
                       in_=ap, deps=[wq_state["rd"][b]])
                t = E("scalar", "activation", out=dmp2[:], in_=buf[:],
                      func=AF.Abs, bias=0.0, scale=1.0,
                      accum_out=absacc[:, base + idx:base + idx + 1], deps=[td])
                wq_state["rd"][b] = t
                wq_state["abslast"] = t

        def wq_ws_compute():
            E("vector", "tensor_reduce", out=wqr[:, 0:1], in_=absacc[:, 0:n1],
              axis=AX.X, op=ALU.add, deps=[wq_state["abslast"]])
            tred = E("vector", "tensor_reduce", out=wqr[:, 1:2],
                     in_=absacc[:, n1:n1 + n2], axis=AX.X, op=ALU.add)
            tsum = E("tensor", "matmul", pD[0:1, 256:258], onescol[:],
                     wqr[:, 0:2], start=True, stop=True,
                     deps=[tok["pDfree"], tred], inc=1)
            tgc = E("vector", "tensor_copy", out=wqr[0:1, 2:4],
                    in_=pD[0:1, 256:258], deps=[tsum])
            E("vector", "tensor_scalar", out=wqr[0:1, 2:3], in0=wqr[0:1, 2:3],
              scalar1=1.0 / (D * H), scalar2=EPS_Q, op0=ALU.mult, op1=ALU.max,
              deps=[tgc])
            tv = E("vector", "tensor_scalar", out=wqr[0:1, 3:4], in0=wqr[0:1, 3:4],
                   scalar1=1.0 / (D * H), scalar2=EPS_Q, op0=ALU.mult, op1=ALU.max)
            E("tensor", "matmul", pD[:, 0:1], onesrow[:], wqr[0:1, 2:3],
              start=True, stop=False, deps=[tok["pDfree"], tv])
            tbc = E("tensor", "matmul", pD[:, 128:129], onesrow[:], wqr[0:1, 3:4],
                    start=False, stop=True, inc=1)
            E("vector", "tensor_copy", out=ws1b[:], in_=pD[:, 0:1], deps=[tbc])
            E("vector", "tensor_copy", out=ws2b[:], in_=pD[:, 128:129])
            E("vector", "tensor_scalar", out=hws1b[:, 0:1], in0=ws1b[:],
              scalar1=0.5, scalar2=None, op0=ALU.mult)
            E("vector", "tensor_scalar", out=hws1b[:, 1:2], in0=hws1b[:, 0:1],
              scalar1=-1.0, scalar2=None, op0=ALU.mult)
            E("vector", "tensor_scalar", out=hws2b[:, 0:1], in0=ws2b[:],
              scalar1=0.5, scalar2=None, op0=ALU.mult)
            E("vector", "tensor_scalar", out=hws2b[:, 1:2], in0=hws2b[:, 0:1],
              scalar1=-1.0, scalar2=None, op0=ALU.mult)
            tv2 = E("vector", "tensor_scalar", out=w2s127[:], in0=ws2b[:],
                    scalar1=1.0 / QB, scalar2=None, op0=ALU.mult)
            wq_state["wsb"] = tv2
            tok["pDfree"] = tv2

        def wq_pass2_slice(items):
            for ap, base, idx in items:
                which = 1 if base == 0 else 2
                hb = hws1b if which == 1 else hws2b
                b = wq_state["n"] % 2
                wq_state["n"] += 1
                buf = wstg if b == 0 else qt
                td = E("sync", "dma_start", out=buf[:],
                       in_=ap, deps=[wq_state["rd"][b]])
                for hf in range(2):
                    bsl = slice(hf * 512, (hf + 1) * 512)
                    p8 = wq_state["wg_n"] % 2
                    wq_state["wg_n"] += 1
                    bge = wg1 if p8 == 0 else wg2
                    ble = wg2 if p8 == 0 else wg1
                    tg1 = E("vector", "tensor_scalar", out=bge[:], in0=buf[:, bsl],
                            scalar1=hb[:, 0:1], scalar2=None, op0=ALU.is_ge,
                            deps=[td, wq_state["wsb"], wq_state["w8"][p8]])
                    tg2 = E("vector", "tensor_scalar", out=ble[:], in0=buf[:, bsl],
                            scalar1=hb[:, 1:2], scalar2=None, op0=ALU.is_le,
                            deps=[wq_state["w8"][1 - p8]])
                    wq_state["rd"][b] = tg2
                    tv = E("vector", "tensor_tensor", out=bge[:], in0=bge[:],
                           in1=ble[:], op=ALU.subtract, deps=[tg1, tg2])
                    if which == 1:
                        nj = H // 1024
                        i, j = idx // nj, idx % nj
                        dst = w1q_d[i * 128:(i + 1) * 128,
                                    j * 1024 + hf * 512:j * 1024 + (hf + 1) * 512]
                    else:
                        dst = w2q_d[idx * 128:(idx + 1) * 128, bsl]
                    wq_state["w8"][p8] = E("gdma", "dma_start", out=dst,
                                           in_=bge[:], deps=[tv])
                    wq_state["w8dma"] = wq_state["w8"][p8]

        p1e = max(1, min(16, n_tot // 2))
        cnts = [0] * n_tot
        nw = min(n_warm, p1e)
        rem = nsub
        if p1e - nw >= 4:
            for c in range(nw):
                cnts[c] = min(rem, 2)
                rem -= cnts[c]
        m1 = list(range(nw, p1e))
        for i, c in enumerate(m1):
            t = -(-rem // (len(m1) - i))
            cnts[c] = t
            rem -= t
        if rem:
            cnts[p1e - 1] += rem
        rem = nsub
        m2 = list(range(p1e, n_tot))
        for i, c in enumerate(m2):
            t = -(-rem // (len(m2) - i))
            cnts[c] = t
            rem -= t

        ln1(0, t_x0)
        front(0)
        o1 = o2 = 0
        for c in range(n_tot):
            if c < p1e:
                items = subs[o1:o1 + cnts[c]]
                o1 += cnts[c]
                kindw = "p1"
            else:
                items = subs[o2:o2 + cnts[c]]
                o2 += cnts[c]
                kindw = "p2"
            q4 = -(-len(items) // 4) if items else 1
            wq_state["mid"] = (kindw if items else None,
                               [items[i:i + q4] for i in range(0, len(items), q4)])
            chunk(c)
            if c >= n_warm:
                ln2_stats(c)
            if c == p1e - 1:
                wq_state["ws_pending"] = True

        tok_wq = wq_state["w8dma"]
        barrA = E.barrier_tokens()
        esA.close()

        esB = ExitStack()
        sbB = lambda name, shape, dt=F32: esB.enter_context(nc.sbuf_tensor(name, shape, dt))
        W1Q = sbB("W1Q", [128, KT * H], FP8)
        W2Q = sbB("W2Q", [128, (H // 128) * D], FP8)
        b1b = sbB("b1b", [128, H])
        b2b = sbB("b2b", [128, D])
        x2b = [sbB("x2b0", [128, D]), sbB("x2b1", [128, D]), sbB("x2b2", [128, D])]
        vq = [sbB("vq0", [128, D]), sbB("vq1", [128, D])]
        tq = [esB.enter_context(nc.sbuf_tensor("tq0", [128, D], I32)),
              esB.enter_context(nc.sbuf_tensor("tq1", [128, D], I32))]
        cbuf = sbB("cbuf", [128, D])
        cbuf2 = sbB("cbuf2", [128, D])
        xq_tm = sbB("xq_tm", [128, D], BF16)
        xqT = sbB("xqT", [128, D], BF16)
        hdeq = [sbB("hdeq0", [128, H]), sbB("hdeq1", [128, H])]
        hq_tm = sbB("hq_tm", [128, H], BF16)
        hqT = sbB("hqT", [128, H], BF16)
        outb = [sbB("outb0", [128, D]), sbB("outb1", [128, D])]
        pH = [esB.enter_context(nc.psum_tensor("pH0", [128, 512], F32)),
              esB.enter_context(nc.psum_tensor("pH1", [128, 512], F32))]
        pO = esB.enter_context(nc.psum_tensor("pO", [128, 1024], F32))
        pT1 = esB.enter_context(nc.psum_tensor("pT1", [128, 2048], BF16))
        pT2 = esB.enter_context(nc.psum_tensor("pT2", [128, 2048], BF16))
        b1row = sbB("b1rowB", [1, H])
        b2row = sbB("b2rowB", [1, D])

        E("sync", "dma_start", out=b1row[:], in_=b1_d[:], deps=barrA)
        t_rowsB = E("sync", "dma_start", out=b2row[:], in_=b2_d[:])
        t_x2b0 = E("sync", "dma_start", out=x2b[0][:], in_=x2s_d[0:128, :])
        w1l, w2l = [], []
        for kt in range(KT):
            w1l.append(E("sync" if kt % 2 == 0 else "gdma", "dma_start",
                         out=W1Q[:, kt * H:(kt + 1) * H],
                         in_=w1q_d[kt * 128:(kt + 1) * 128, :], deps=barrA))
        for kt in range(H // 128):
            w2l.append(E("sync" if kt % 2 == 0 else "gdma", "dma_start",
                         out=W2Q[:, kt * D:(kt + 1) * D],
                         in_=w2q_d[kt * 128:(kt + 1) * 128, :], deps=barrA))

        tprev = None
        for hh in range(KT):
            sl = slice(hh * 512, (hh + 1) * 512)
            tt_b1 = E("tensor", "matmul", pH[hh % 2][:], onesrow[:], b1row[:, sl],
                      start=True, stop=True,
                      deps=barrA + [t_rowsB] + ([tprev] if tprev else []), inc=1)
            tprev = E("vector", "tensor_copy", out=b1b[:, sl], in_=pH[hh % 2][:],
                      deps=[tt_b1])
        for hh in range(2):
            sl = slice(hh * 512, (hh + 1) * 512)
            tt_b2 = E("tensor", "matmul", pO[:, sl], onesrow[:], b2row[:, sl],
                      start=True, stop=True, inc=1)
        tv_b2 = E("vector", "tensor_copy", out=b2b[:], in_=pO[:], deps=[tt_b2])

        E.strict_scalar = False
        tokB = {
            "trx": None, "trh_t": [None, None],
            "s_xqT": tprev, "s_hqT1": tprev, "s_hqT2": tprev,
            "dq": [tprev, tprev], "epi": [tv_b2, tv_b2],
            "x2rd": [None, None, None], "ydma": [None, None],
            "x2w": [t_x2b0, None, None],
            "h1last": None, "o": tv_b2, "tv_hq": None,
            "chain_last": [None, None],
            "chain_s1": [None, None], "chain_s2": [None, None],
            "rqh": None, "hql01": None, "lead_last": None, "chain_tsc": None,
            "tv_hq01": None,
        }

        def quant_chain(par, src_ap, sca, scb, n, out_bf, war_dep, deps0=(),
                        leads=None):
            if leads is not None:
                tv0, tsc = leads
            else:
                d0 = list(deps0)
                tv0 = E("scalar", "activation", out=vq[par][:, 0:n], in_=src_ap,
                        func=AF.Identity, scale=sca,
                        bias=(scb if scb is not None else 0.0),
                        deps=d0 + [tokB["chain_s1"][par]])
                tsc = E("scalar", "activation", out=tq[par][:, 0:n], in_=src_ap,
                        func=AF.Identity, scale=sca,
                        bias=(scb if scb is not None else 0.0),
                        deps=d0 + [tokB["chain_s2"][par]])
                tokB["chain_tsc"] = tsc
            E("vector", "scalar_tensor_tensor", out=cbuf[:, 0:n],
              in0=tq[par][:, 0:n], scalar=0.5, in1=vq[par][:, 0:n],
              op0=ALU.add, op1=ALU.is_le, deps=[tv0, tsc])
            tokB["chain_s1"][par] = E("vector", "scalar_tensor_tensor",
                                      out=cbuf2[:, 0:n],
                                      in0=tq[par][:, 0:n], scalar=-0.5,
                                      in1=vq[par][:, 0:n],
                                      op0=ALU.add, op1=ALU.is_ge)
            tokB["chain_s2"][par] = E("vector", "scalar_tensor_tensor",
                                      out=cbuf[:, 0:n],
                                      in0=cbuf[:, 0:n], scalar=1.0,
                                      in1=tq[par][:, 0:n],
                                      op0=ALU.mult, op1=ALU.add)
            t = E("vector", "tensor_tensor", out=out_bf, in0=cbuf[:, 0:n],
                  in1=cbuf2[:, 0:n], op=ALU.subtract, deps=[war_dep])
            tokB["chain_last"][par] = t
            return t

        def emit_prefetch(mq):
            if 0 < mq < n_main:
                g3 = mq % 3
                tokB["x2w"][g3] = E("sync", "dma_start", out=x2b[g3][:],
                                    in_=x2s_d[mq * 128:(mq + 1) * 128, :],
                                    deps=[tokB["x2rd"][g3]])


        def emit_hq_leads(mh):
            if 0 <= mh < n_main:
                p = mh % 2
                tokB["hql01"] = []
                for sub in range(2):
                    par = (sub + 1) % 2
                    sl = slice(sub * 1024, (sub + 1) * 1024)
                    tv0 = E("scalar", "activation", out=vq[par][:],
                            in_=hdeq[p][:, sl], func=AF.Identity,
                            scale=r2[:, 2:3], bias=0.0,
                            deps=[tokB["rqh"], tokB["chain_s1"][par]])
                    tsc = E("scalar", "activation", out=tq[par][:],
                            in_=hdeq[p][:, sl], func=AF.Identity,
                            scale=r2[:, 2:3], bias=0.0,
                            deps=[tokB["rqh"], tokB["chain_s2"][par]])
                    tokB["hql01"].append((tv0, tsc))

        def emit_chains(mf, mh):
            if 0 <= mh < n_main:
                p = mh % 2
                for sub in range(4):
                    sl = slice(sub * 1024, (sub + 1) * 1024)
                    tokB["tv_hq"] = quant_chain(
                        (sub + 1) % 2, hdeq[p][:, sl], r2[:, 2:3], None, 1024,
                        hq_tm[:, sl], tokB["trh_t"][0 if sub < 2 else 1],
                        deps0=[tokB["rqh"]],
                        leads=tokB["hql01"][sub] if sub < 2 else None)
                    if sub == 1:
                        tokB["tv_hq01"] = tokB["tv_hq"]
                    if sub >= 2:
                        tokB["lead_last"] = tokB["chain_tsc"]

        def emit_front(mf):
            if not (0 <= mf < n_main):
                return
            pf = mf % 2
            rpartf = r2 if pf == 0 else r1
            sdq1c = r2[:, 0:1] if pf == 0 else r2[:, 14:15]
            for dt_ in range(KT):
                tok_t = E("tensor", "transpose",
                          pT2[:, dt_ * 128:(dt_ + 1) * 128],
                          xq_tm[:, dt_ * 128:(dt_ + 1) * 128], idenb[:],
                          deps=[tokB["xqw"], tokB["s_hqT2"]] if dt_ == 0 else (),
                          inc=1 if dt_ == KT - 1 else 0)
            tokB["trx"] = tok_t
            if mf + 1 < n_main:
                tokB["xqw"] = E("gdma", "dma_start", out=xq_tm[:],
                                in_=xqs_d[(mf + 1) * 128:(mf + 2) * 128, :],
                                deps=[tok_t])
            tokB["s_xqT"] = E("scalar", "copy", out=xqT[:], in_=pT2[:, 0:1024],
                              deps=[tok_t, tokB["h1last"]])
            tv_s1 = E("vector", "tensor_scalar", out=sdq1c,
                      in0=stats[:, 3 * mf + 2:3 * mf + 3],
                      scalar1=ws1b[:, 0:1], scalar2=None, op0=ALU.mult)
            tgel = {}
            def part_max(hh):
                E("vector", "tensor_reduce", out=rpartf[:, 5 + hh:6 + hh],
                  in_=hdeq[pf][:, hh * 512:(hh + 1) * 512], axis=AX.X,
                  op=ALU.max, apply_absolute_value=True, deps=[tgel[hh]])
            tg = None
            for hh in range(KT):
                pg = hh % 2
                for dt_ in range(KT):
                    d_ = ([tokB["s_xqT"], tokB["dq"][pg]] if dt_ == 0 else [])
                    if mf == 0 and hh == 0:
                        d_ = d_ + [w1l[dt_]]
                    tok_t = E("tensor", "matmul", pH[pg][:],
                              xqT[:, dt_ * 128:(dt_ + 1) * 128],
                              W1Q[:, dt_ * H + hh * 512: dt_ * H + hh * 512 + 512],
                              start=(dt_ == 0), stop=(dt_ == KT - 1),
                              deps=d_,
                              inc=1 if dt_ == KT - 1 else 0)
                sl = slice(hh * 512, (hh + 1) * 512)
                tokB["dq"][pg] = E("vector", "scalar_tensor_tensor",
                                   out=hdeq[pf][:, sl], in0=pH[pg][:],
                                   scalar=sdq1c, in1=b1b[:, sl],
                                   op0=ALU.mult, op1=ALU.add,
                                   deps=[tok_t, tv_s1])
                tg = E("scalar", "activation", out=hdeq[pf][:, sl],
                       in_=hdeq[pf][:, sl], func=AF.Gelu, bias=0.0,
                       scale=1.0, deps=[tokB["dq"][pg], tg])
                tgel[hh] = tg
                if hh >= 1:
                    part_max(hh - 1)
            part_max(KT - 1)
            tokB["h1last"] = tok_t
            E("vector", "tensor_reduce", out=r2[:, 1:2], in_=rpartf[:, 5:13],
              axis=AX.X, op=ALU.max)
            E("vector", "tensor_scalar", out=r2[:, 1:2], in0=r2[:, 1:2],
              scalar1=EPS_Q, scalar2=None, op0=ALU.max)
            E("vector", "reciprocal", out=r2[:, 2:3], in_=r2[:, 1:2],
              deps=[tokB["lead_last"]])
            tokB["rqh"] = E("vector", "tensor_scalar", out=r2[:, 2:3],
                            in0=r2[:, 2:3], scalar1=QB, scalar2=None,
                            op0=ALU.mult)
            E("vector", "tensor_scalar", out=r2[:, 3 + pf:4 + pf],
              in0=r2[:, 1:2], scalar1=w2s127[:, 0:1], scalar2=None,
              op0=ALU.mult)

        tokB["xqw"] = E("gdma", "dma_start", out=xq_tm[:], in_=xqs_d[0:128, :],
                        deps=barrA)
        emit_chains(0, -1)
        emit_front(0)
        emit_prefetch(1)

        for m in range(n_main + 1):
            emit_hq_leads(m)
            if m >= 1:
                for grp in range(2):
                    pdst = pT1 if grp == 0 else pT2
                    dep0 = ([tokB["tv_hq01"], tokB["s_hqT1"]] if grp == 0
                            else [tokB["tv_hq"], tokB["s_xqT"]])
                    for j in range(16):
                        dt_ = grp * 16 + j
                        tok_t = E("tensor", "transpose",
                                  pdst[:, j * 128:(j + 1) * 128],
                                  hq_tm[:, dt_ * 128:(dt_ + 1) * 128], idenb[:],
                                  deps=dep0 if j == 0 else (),
                                  inc=1 if j == 15 else 0)
                    tokB["trh_t"][grp] = tok_t
                    if grp == 0:
                        tokB["s_hqT1"] = E("scalar", "copy", out=hqT[:, 0:2048],
                                           in_=pT1[:], deps=[tok_t, tokB["o"]])
                    else:
                        tokB["s_hqT2"] = E("scalar", "copy", out=hqT[:, 2048:4096],
                                           in_=pT2[:], deps=[tok_t, tokB["o"]])

            if m >= 1:
                mp = m - 1
                gp = mp % 3
                gpy = mp % 2
                for hh in range(2):
                    sl = slice(hh * 512, (hh + 1) * 512)
                    for kk in range(H // 128):
                        if kk == 0:
                            d_ = [tokB["s_hqT1"], tokB["epi"][hh]]
                        elif kk == 16:
                            d_ = [tokB["s_hqT2"]]
                        else:
                            d_ = []
                        if mp == 0 and hh == 0:
                            d_ = d_ + [w2l[kk]]
                        tok_t = E("tensor", "matmul", pO[:, sl],
                                  hqT[:, kk * 128:(kk + 1) * 128],
                                  W2Q[:, kk * D + hh * 512: kk * D + hh * 512 + 512],
                                  start=(kk == 0), stop=(kk == H // 128 - 1),
                                  deps=d_,
                                  inc=1 if kk == H // 128 - 1 else 0)
                tokB["o"] = tok_t
            emit_chains(m + 1, m)
            if m >= 1:
                for hh in range(2):
                    sl = slice(hh * 512, (hh + 1) * 512)
                    E("vector", "scalar_tensor_tensor", out=outb[gpy][:, sl],
                      in0=pO[:, sl], scalar=r2[:, 3 + gpy:4 + gpy],
                      in1=x2b[gp][:, sl], op0=ALU.mult, op1=ALU.add,
                      deps=[tokB["o"], tokB["ydma"][gpy], tokB["x2w"][gp]])
                    tokB["epi"][hh] = E("vector", "tensor_tensor",
                                        out=outb[gpy][:, sl], in0=outb[gpy][:, sl],
                                        in1=b2b[:, sl], op=ALU.add)
                tokB["x2rd"][gp] = tokB["epi"][1]
                tokB["ydma"][gpy] = E("sync", "dma_start",
                                      out=y_d[mp * 128:(mp + 1) * 128, :],
                                      in_=outb[gpy][:], deps=[tokB["epi"][1]])
            emit_prefetch(m + 2)
            emit_front(m + 1)

        esB.close()
        E.final_wait("sync")
        E.flush(block)
    return nc


def shard_inputs(x, Wq, bq, Wk, bk, Wv, bv, gamma, W1, b1, W2, b2,
                 n_span=4, n_warm=4):
    f32 = np.float32
    B, S, D_ = x.shape
    span = S // n_span
    W = n_warm * C
    x = np.ascontiguousarray(np.asarray(x, dtype=f32))
    common = {
        "wqt": np.ascontiguousarray(np.asarray(Wq, f32).T),
        "wkt": np.ascontiguousarray(np.asarray(Wk, f32).T),
        "wvt": np.ascontiguousarray(np.asarray(Wv, f32).T),
        "bq": np.asarray(bq, f32).reshape(1, -1),
        "bk": np.asarray(bk, f32).reshape(1, -1),
        "bv": np.asarray(bv, f32).reshape(1, -1),
        "gam": np.asarray(gamma, f32).reshape(1, 1),
        "w1t": np.ascontiguousarray(np.asarray(W1, f32).T),
        "b1": np.asarray(b1, f32).reshape(1, -1),
        "w2t": np.ascontiguousarray(np.asarray(W2, f32).T),
        "b2": np.asarray(b2, f32).reshape(1, -1),
    }
    in_maps = []
    for core in range(B * n_span):
        b = core // n_span
        s = core % n_span
        if s == 0:
            xs = np.concatenate([np.zeros((W, D_), f32), x[b, 0:span]], axis=0)
            wm = np.zeros((128, 1), f32)
        else:
            xs = x[b, s * span - W:(s + 1) * span]
            wm = np.ones((128, 1), f32)
        mm = dict(common)
        mm["x"] = np.ascontiguousarray(xs)
        mm["wmask"] = wm
        in_maps.append(mm)
    return in_maps, span


def kernel(L0_attn_x_seq, L0_attn_Wq, L0_attn_bq, L0_attn_Wk, L0_attn_bk,
           L0_attn_Wv, L0_attn_bv, L0_attn_gamma, L0_ffn_W1, L0_ffn_b1,
           L0_ffn_W2, L0_ffn_b2):
    n_span, n_warm = 4, 3
    B, S, D_ = L0_attn_x_seq.shape
    in_maps, span = shard_inputs(
        L0_attn_x_seq, L0_attn_Wq, L0_attn_bq, L0_attn_Wk, L0_attn_bk,
        L0_attn_Wv, L0_attn_bv, L0_attn_gamma, L0_ffn_W1, L0_ffn_b1,
        L0_ffn_W2, L0_ffn_b2, n_span=n_span, n_warm=n_warm)
    nc = build_nc(n_main=span // C, n_warm=n_warm)
    res = run_bass_kernel_spmd(nc, in_maps, core_ids=list(range(8)))
    out = np.empty((B, S, D_), dtype=np.float32)
    for core in range(8):
        b = core // n_span
        s = core % n_span
        out[b, s * span:(s + 1) * span] = res.results[core]["y"]
    return out
